# version 14
# baseline (speedup 1.0000x reference)
import os
from contextlib import ExitStack

import numpy as np
import concourse.bass as bass
import concourse.mybir as mybir
from concourse.bass_utils import run_bass_kernel_spmd
from concourse import library_config
from concourse.library_overlay import lower_extended_insts

F32 = mybir.dt.float32
F32R = mybir.dt.float32r
BF16 = mybir.dt.bfloat16
I16 = mybir.dt.int16
AF = mybir.ActivationFunctionType
ALU = mybir.AluOpType
AX = mybir.AxisListType

T, D = 2048, 1024
NC = 16
D2 = 2048
STOP = int(os.environ.get("MK_STOP", "6"))


class Sem:
    def __init__(self, nc, stack, name):
        self.h = stack.enter_context(nc.semaphore(name))
        self.n = 0

    def inc(self, instr, k=1):
        instr.then_inc(self.h, k)
        self.n += k


def _build():
    nc = bass.Bass("TRN2", target_bir_lowering=False)

    def din(name, shape, dtype=F32):
        return nc.dram_tensor(name, list(shape), dtype, kind="ExternalInput")

    x_d = din("x", [T, D])
    w0t_d = din("w0t", [D, D])
    normw_d = din("normw", [1, D])
    wt0_d = {k: din("w%st0" % k, [D, D]) for k in "qkvo"}
    wt1_d = {k: din("w%st1" % k, [D, D]) for k in "qkvo"}
    bq0f_d = din("bq0f", [128, 8]); bk0f_d = din("bk0f", [128, 8])
    bq1f_d = din("bq1f", [128, 8]); bk1f_d = din("bk1f", [128, 8])
    bv0r_d = din("bv0r", [1, D]); bo0r_d = din("bo0r", [1, D])
    bv1r_d = din("bv1r", [1, D]); bo1r_d = din("bo1r", [1, D])
    wq8k_d = din("wq8k", [8192, D])
    keyst_d = din("keyst", [2, 8, 512, 128])
    wint_d = din("wint", [D, D2])
    woutt_d = din("woutt", [D2, D])
    tin_d = din("tin", [16384, 3072])
    tout_d = din("tout", [16384, 3072])

    out_d = nc.dram_tensor("out", [T, D], F32, kind="ExternalOutput")
    dbg_d = (nc.dram_tensor("dbg", [128, 64], F32, kind="ExternalOutput")
             if os.environ.get("MK_DBG") else None)
    mdram = nc.dram_tensor("mdram", [1024, 2048], BF16, kind="Internal")
    xsp1 = nc.dram_tensor("xsp1", [8, 128, 1024], F32, kind="Internal")
    xsp2 = nc.dram_tensor("xsp2", [8, 128, 1024], F32, kind="Internal")

    stack = ExitStack()

    def sbuf(name, shape, dtype):
        return stack.enter_context(nc.sbuf_tensor(name, list(shape), dtype))

    def psum(name, shape, dtype):
        return stack.enter_context(nc.psum_tensor(name, list(shape), dtype))

    X = sbuf("X", [128, NC, D], F32)
    U2 = sbuf("U2", [128, 8192], F32)
    U3 = sbuf("U3", [128, 8192], F32)
    U4 = sbuf("U4", [128, 8192], F32)
    U5 = sbuf("U5", [128, 6144], F32)
    U6 = sbuf("U6", [128, 4096], F32)
    identb = sbuf("identb", [128, 128], BF16)
    i128f = sbuf("i128f", [128, 128], F32)
    causal = sbuf("causal", [128, 128], F32)
    iotaB = sbuf("iotaB", [128, 128], F32)
    permrep = sbuf("permrep", [128, 8, 128], BF16)
    onesb = sbuf("onesb", [1, 128], BF16)
    bfm = sbuf("bfm", [128, 4, 8], F32)
    bfms = sbuf("bfms", [128, 4, 8], F32)
    mv = sbuf("mv", [128, NC, 2], F32)
    stat6 = sbuf("stat6", [128, 2, 6], F32)
    stdt = sbuf("stdt", [128, NC], F32)
    scaleA = sbuf("scaleA", [128, NC], F32)
    biasA = sbuf("biasA", [128, NC], F32)
    scaleB = sbuf("scaleB", [128, NC], F32)
    biasB = sbuf("biasB", [128, NC], F32)
    tmpc = sbuf("tmpc", [128, NC], F32)
    lin1 = sbuf("lin1", [128, 1], F32)
    gate8 = sbuf("gate8", [128, 2, 8], F32)
    m16 = sbuf("m16", [128, 16], F32)
    a16 = sbuf("a16", [128, 16], F32)
    idxf = sbuf("idxf", [128, 8], F32)
    wrap16 = sbuf("wrap16", [128, 2, 64], I16)
    a16b_v = sbuf("a16b", [128, 16], BF16)
    wtmp_v = sbuf("wtmp", [128, 64], F32)
    diagt = sbuf("diagt", [128, 128], F32)
    rs16 = sbuf("rs16", [128, 16], F32)
    rc16 = sbuf("rc16", [128, 16], F32)
    nmax = sbuf("nmax", [128, 2], F32)
    fdum = sbuf("fdum", [1, 2], F32)

    gnum = stack.enter_context(nc.gpsimd.register("gnum"))
    ps0 = psum("ps0", [128, 2048], F32)
    ps1 = psum("ps1", [128, 2048], F32)

    sems = {nm: Sem(nc, stack, "s_" + nm)
            for nm in ["dA", "dB", "dC", "dD", "dG", "dH", "dI", "dJ", "dO",
                       "sS", "sV", "sW", "sT", "sU", "sQ"]}
    dA, dB, dG, dO = sems["dA"], sems["dB"], sems["dG"], sems["dO"]
    dC, dD, dH = sems["dC"], sems["dD"], sems["dH"]
    dI, dJ = sems["dI"], sems["dJ"]
    sS, sV, sW, sT, sU, sQ = (sems["sS"], sems["sV"], sems["sW"],
                              sems["sT"], sems["sU"], sems["sQ"])

    with nc.Block() as blk:
        @blk.gpsimd
        def _(g):
            def fence(instr):
                sQ.inc(instr)
                g.wait_ge(sQ.h, sQ.n)
            for tile in (identb, i128f):
                fence(g.memset(tile[:, :], 1.0))
                g.affine_select(out=tile[:, :], in_=tile[:, :],
                                compare_op=ALU.is_equal, fill=0.0, base=0,
                                pattern=[[-1, 128]], channel_multiplier=1)
            fence(g.memset(causal[:, :], 0.0))
            g.affine_select(out=causal[:, :], in_=causal[:, :],
                            compare_op=ALU.is_ge, fill=-1e30, base=0,
                            pattern=[[-1, 128]], channel_multiplier=1)
            g.iota(iotaB[:, :], [[1, 128]], base=256, channel_multiplier=0,
                   allow_small_or_imprecise_dtypes=True)
            for r in range(8):
                sl = permrep[:, r, :]
                fence(g.memset(sl, 1.0))
                g.affine_select(out=sl, in_=sl, compare_op=ALU.is_equal,
                                fill=0.0, base=-16 * r,
                                pattern=[[0, 8], [-1, 16]],
                                channel_multiplier=1)
            g.memset(onesb[:, :], 1.0)
            g.reg_mov(gnum, 128)
            g.load_library(library_config.mlp)

    with nc.Block() as blk:
        @blk.sync
        def _(e):
            for dst, src in [
                (bfm[:, 0, :], bq0f_d.ap()), (bfm[:, 1, :], bk0f_d.ap()),
                (bfm[:, 2, :], bq1f_d.ap()), (bfm[:, 3, :], bk1f_d.ap()),
            ]:
                dA.inc(e.dma_start(dst, src), 16)
            for c in range(NC):
                dA.inc(e.dma_start(X[:, c, :], x_d[c * 128:(c + 1) * 128, :]), 16)
            e.wait_ge(dA.h, dA.n)
        @blk.vector
        def _(v):
            v.wait_ge(dA.h, dA.n)
            v.tensor_scalar_mul(bfms[:, 0, :], bfm[:, 0, :], 0.125)
            v.tensor_scalar_mul(bfms[:, 1, :], bfm[:, 1, :], 1.0)
            v.tensor_scalar_mul(bfms[:, 2, :], bfm[:, 2, :], 1024.0 ** -0.5)
            v.tensor_scalar_mul(bfms[:, 3, :], bfm[:, 3, :], 1.0)

    wq_st = U5[:, 0:4096].rearrange("p (a b) -> p a b", a=4)
    ky_st = U5[:, 4096:4608].rearrange("p (a b) -> p a b", a=4)
    wq_b = U6[:, 0:2048].bitcast(BF16).rearrange("p (a b) -> p a b", a=4)
    ky_b = U6[:, 2048:2304].bitcast(BF16).rearrange("p (a b) -> p a b", a=4)
    m_ev = U6[:, 2304:2432].bitcast(BF16).rearrange("p (a b) -> p a b", a=2)
    with nc.Block() as blk:
        b_dB, b_sV, b_sT, b_sU = dB.n, sV.n, sT.n, sU.n
        b_dG, b_dH = dG.n, dH.n

        @blk.sync
        def _(e):
            for ph in range(16):
                if ph >= 1:
                    e.wait_ge(sV.h, b_sV + ph)
                for kd in range(4):
                    dB.inc(e.dma_start(
                        wq_st[:, kd, :],
                        wq8k_d[ph * 512 + kd * 128: ph * 512 + (kd + 1) * 128, :]), 16)
                    dB.inc(e.dma_start(
                        ky_st[:, kd, :],
                        keyst_d[ph // 8, ph % 8, kd * 128:(kd + 1) * 128, :]), 16)
        @blk.vector
        def _(v):
            for ph in range(16):
                v.wait_ge(dB.h, b_dB + 128 * (ph + 1))
                v.tensor_copy(wq_b[:, :, :], wq_st[:, :, :])
                sV.inc(v.tensor_copy(ky_b[:, :, :], ky_st[:, :, :]))
                for m in range(8):
                    k = ph * 8 + m
                    v.wait_ge(sT.h, b_sT + k + 1)
                    if k >= 2:
                        dpar = dG if k % 2 == 0 else dH
                        v.wait_ge(dpar.h, (b_dG if k % 2 == 0 else b_dH)
                                  + 16 * (k // 2))
                    sU.inc(v.tensor_copy(
                        m_ev[:, m % 2, :],
                        ps0[:, 512 * (m % 2):512 * (m % 2) + 128]))
        @blk.tensor
        def _(t):
            for ph in range(16):
                t.wait_ge(sV.h, b_sV + ph + 1)
                for m in range(8):
                    if ph * 8 + m >= 2:
                        t.wait_ge(sU.h, b_sU + ph * 8 + m - 1)
                    pm = ps0[:, 512 * (m % 2):512 * (m % 2) + 128]
                    for kd in range(4):
                        ins = t.matmul(pm, wq_b[:, kd, m * 128:(m + 1) * 128],
                                       ky_b[:, kd, :],
                                       start=(kd == 0), stop=(kd == 3))
                    sT.inc(ins)
        @blk.gpsimd
        def _(g):
            for ph in range(16):
                for m in range(8):
                    g.wait_ge(sU.h, b_sU + ph * 8 + m + 1)
                    dpar = dG if m % 2 == 0 else dH
                    dpar.inc(g.dma_start(
                        mdram[m * 128:(m + 1) * 128, ph * 128:(ph + 1) * 128],
                        m_ev[:, m % 2, :]), 16)
            g.wait_ge(dG.h, dG.n)
            g.wait_ge(dH.h, dH.n)

    def ln_alpha(with_rms=False):
        with nc.Block() as blk:
            b_sV, b_sS = sV.n, sS.n

            @blk.vector
            def _(v):
                def vf(instr):
                    sW.inc(instr)
                    v.wait_ge(sW.h, sW.n)
                for c in range(NC):
                    xc = X[:, c, :]
                    v.wait_ge(sV.h, b_sV + c)
                    v.bn_stats(stat6[:, 0, :], xc[:, 0:512])
                    vf(v.bn_stats(stat6[:, 1, :], xc[:, 512:1024]))
                    sV.inc(v.bn_aggr(mv[:, c, :], stat6[:, :, :]))
                for c in range(NC):
                    v.wait_ge(sS.h, b_sS + c + 1)
                    vf(v.reciprocal(scaleA[:, c:c + 1], stdt[:, c:c + 1]))
                    v.tensor_scalar(biasA[:, c:c + 1], mv[:, c, 0:1],
                                    scaleA[:, c:c + 1], -1.0,
                                    op0=ALU.mult, op1=ALU.mult)
                if with_rms:
                    for c in range(NC):
                        v.tensor_scalar(tmpc[:, c:c + 1], mv[:, c, 1:2],
                                        scaleA[:, c:c + 1], scaleA[:, c:c + 1],
                                        op0=ALU.mult, op1=ALU.mult)
                    sV.inc(v.memset(fdum[0:1, 0:1], 0.0))
                    for c in range(NC):
                        v.wait_ge(sS.h, b_sS + NC + c + 1)
                        vf(v.reciprocal(tmpc[:, c:c + 1], stdt[:, c:c + 1]))
                    sV.inc(v.tensor_tensor(scaleB[:, :], scaleA[:, :], tmpc[:, :],
                                           op=ALU.mult))
                    v.tensor_tensor(biasB[:, :], biasA[:, :], tmpc[:, :],
                                    op=ALU.mult)
            @blk.scalar
            def _(s):
                for c in range(NC):
                    s.wait_ge(sV.h, b_sV + c + 1)
                    sS.inc(s.activation(stdt[:, c:c + 1], mv[:, c, 1:2],
                                        AF.Sqrt, bias=epsln_t[:, :]))
                if with_rms:
                    s.wait_ge(sV.h, b_sV + NC + 1)
                    for c in range(NC):
                        sS.inc(s.activation(stdt[:, c:c + 1], tmpc[:, c:c + 1],
                                            AF.Sqrt, bias=epsrm_t[:, :]))
    epsln_t = sbuf("epsln", [128, 1], F32)
    epsrm_t = sbuf("epsrm", [128, 1], F32)

    with nc.Block() as blk:
        @blk.vector
        def _(v):
            v.memset(epsln_t[:, :], 1e-5)
            v.memset(epsrm_t[:, :], 1e-6)

    sW0 = U4[:, 0:4096].bitcast(BF16).rearrange("p (a b) -> p a b", a=8)
    w0st = U5[:, 0:2048].rearrange("p (a b) -> p a b", a=2)

    def sign_ffn():
        ln_alpha(False)
        with nc.Block() as blk:
            b_dC, b_dD, b_sS = dC.n, dD.n, sS.n

            @blk.sync
            def _(e):
                for j in range(8):
                    if j >= 2:
                        e.wait_ge(sS.h, b_sS + j - 1)
                    dpar = dC if j % 2 == 0 else dD
                    dpar.inc(e.dma_start(w0st[:, j % 2, :],
                                         w0t_d[j * 128:(j + 1) * 128, :]), 16)
            @blk.scalar
            def _(s):
                for j in range(8):
                    s.wait_ge((dC if j % 2 == 0 else dD).h,
                              (b_dC if j % 2 == 0 else b_dD) + 16 * (j // 2 + 1))
                    sS.inc(s.activation(sW0[:, j, :], w0st[:, j % 2, :], AF.Sign))
        xnb = U6[:, 0:1024].bitcast(BF16).rearrange("p (a b) -> p a b", a=2)
        xnfm = U6[:, 1024:2048].bitcast(BF16).rearrange("p (a b c) -> p a b c",
                                                        a=2, b=8)
        with nc.Block() as blk:
            b_sS, b_sT, b_sV, b_sU, b_sW = sS.n, sT.n, sV.n, sU.n, sW.n

            @blk.scalar
            def _(s):
                for c in range(NC):
                    if c >= 2:
                        s.wait_ge(sT.h, b_sT + 2 * (c - 1))
                    sS.inc(s.activation(xnb[:, c % 2, :], X[:, c, :], AF.Identity,
                                        bias=biasA[:, c:c + 1],
                                        scale=scaleA[:, c:c + 1]))
            @blk.tensor
            def _(t):
                for c in range(NC):
                    t.wait_ge(sS.h, b_sS + c + 1)
                    if c >= 2:
                        t.wait_ge(sV.h, b_sV + c - 1)
                    ptb = ps1[:, 512 * (c % 2):512 * (c % 2) + 512].bitcast(BF16)
                    for j in range(8):
                        ins = t.transpose(ptb[:, j * 128:(j + 1) * 128],
                                          xnb[:, c % 2, j * 128:(j + 1) * 128],
                                          identb[:, :])
                    sT.inc(ins)
                    t.wait_ge(sV.h, b_sV + c + 1)
                    if c >= 2:
                        t.wait_ge(sU.h, b_sU + c - 1)
                    py = ps0[:, 1024 * (c % 2):1024 * (c % 2) + 1024]
                    for j in range(8):
                        for n in range(2):
                            ins = t.matmul(py[:, n * 512:(n + 1) * 512],
                                           xnfm[:, c % 2, j, :],
                                           sW0[:, j, n * 512:(n + 1) * 512],
                                           start=(j == 0), stop=(j == 7))
                    sT.inc(ins)
            @blk.vector
            def _(v):
                for c in range(NC):
                    v.wait_ge(sT.h, b_sT + 2 * c + 1)
                    ptb = ps1[:, 512 * (c % 2):512 * (c % 2) + 512].bitcast(BF16)
                    sV.inc(v.tensor_copy(
                        xnfm[:, c % 2, :, :].rearrange("p a b -> p (a b)"),
                        ptb[:, :]))
                    v.wait_ge(sT.h, b_sT + 2 * c + 2)
                    py = ps0[:, 1024 * (c % 2):1024 * (c % 2) + 1024]
                    sU.inc(v.scalar_tensor_tensor(X[:, c, :], py[:, :], 0.5,
                                                  X[:, c, :],
                                                  op0=ALU.mult, op1=ALU.add))

    browb_r = [U5[0:1, 2048 + 512 * i:2048 + 512 * (i + 1)].bitcast(BF16)
               for i in range(3)]
    brow_st = U5[0:1, 0:1024]
    normbc_v = U5[:, 3584:4608]
    bobc_v = U5[:, 4608:5632]
    wst_v = U5[:, 0:2048].rearrange("p (a b) -> p a b", a=2)
    P_sb = U5[:, 0:1024].bitcast(BF16)
    PT_sb = U5[:, 1024:2048].bitcast(BF16)
    w_a = U4[:, 0:4096].bitcast(BF16).rearrange("p (a b) -> p a b", a=8)
    w_b = U4[:, 4096:8192].bitcast(BF16).rearrange("p (a b) -> p a b", a=8)
    w_c, w_d = w_a, w_b
    at_xf = U6[:, 0:1024]
    at_xb = U6[:, 1024:1536].bitcast(BF16)
    at_xfm = U6[:, 1536:2048].bitcast(BF16).rearrange("p (a b) -> p a b", a=8)
    at_q = U6[:, 2048:2560].bitcast(BF16).rearrange("p (a b) -> p a b", a=8)
    at_o = U6[:, 2560:3072].bitcast(BF16)
    at_ofm = U6[:, 3072:3584].bitcast(BF16).rearrange("p (a b) -> p a b", a=8)
    K_fm = U2[:, :].bitcast(BF16).rearrange("p (a b) -> p a b", a=8)
    V_tm = U3[:, :].bitcast(BF16).rearrange("p (a b) -> p a b", a=16)

    def load_weight_pair(wd1, dst1, wd2, dst2, with_rows, l1):
        with nc.Block() as blk:
            b_dC, b_dD, b_sV = dC.n, dD.n, sV.n

            @blk.sync
            def _(e):
                for j in range(16):
                    wd, dst = (wd1, dst1) if j < 8 else (wd2, dst2)
                    jj = j % 8
                    if j >= 2:
                        e.wait_ge(sV.h, b_sV + j - 1)
                    dpar = dC if j % 2 == 0 else dD
                    dpar.inc(e.dma_start(wst_v[:, j % 2, :],
                                         wd[jj * 128:(jj + 1) * 128, :]), 16)
                if with_rows:
                    b_sV2 = sV.n
                    for i, rd in enumerate([bv0r_d, bv1r_d, bo1r_d]):
                        e.wait_ge(sV.h, b_sV2 + 16 + i)
                        dB.inc(e.dma_start(brow_st[:, :], rd.ap()), 16)
                    dA.inc(e.dma_start(normbc_v[0:1, :], normw_d.ap()), 16)
                    dA.inc(e.dma_start(bobc_v[0:1, :], bo0r_d.ap()), 16)
                e.wait_ge(dA.h, dA.n)
                e.wait_ge(dB.h, dB.n)
            @blk.vector
            def _(v):
                for j in range(16):
                    dst = dst1 if j < 8 else dst2
                    v.wait_ge((dC if j % 2 == 0 else dD).h,
                              (b_dC if j % 2 == 0 else b_dD) + 16 * (j // 2 + 1))
                    sV.inc(v.tensor_copy(dst[:, j % 8, :], wst_v[:, j % 2, :]))
                if with_rows:
                    b_dB2 = dB.n - 48
                    for i in range(3):
                        v.wait_ge(dB.h, b_dB2 + 16 * (i + 1))
                        sV.inc(v.tensor_copy(browb_r[i][:, :], brow_st[:, :]))
            if with_rows:
                @blk.gpsimd
                def _(g):
                    g.wait_ge(dA.h, dA.n)
                    g.partition_broadcast(normbc_v[:, :], normbc_v[:, :],
                                          channels=128)
                    if not l1:
                        g.partition_broadcast(bobc_v[:, :], bobc_v[:, :],
                                              channels=128)

    def attn(l1):
        ln_alpha(with_rms=not l1)
        scl = scaleB if not l1 else scaleA
        bia = biasB if not l1 else biasA
        wt = wt1_d if l1 else wt0_d
        qsc_i = 2 if l1 else 0
        ksc_i = 3 if l1 else 1
        vrow = 1 if l1 else 0
        qscale = (1024.0 ** -0.5) if l1 else 0.125
        xb2 = [U6[:, 0:512].bitcast(BF16), U6[:, 512:1024].bitcast(BF16)]
        q2 = [U6[:, 2048:2560].bitcast(BF16).rearrange("p (a b) -> p a b", a=8),
              U6[:, 3584:4096].bitcast(BF16).rearrange("p (a b) -> p a b", a=8)]
        load_weight_pair(wt["k"], w_a, wt["v"], w_b, with_rows=True, l1=l1)
        with nc.Block() as blk:
            for c in range(NC):
                @blk.scalar
                def _(s, c=c):
                    s.wait_ge(sV.h, sV.n)
                    s.wait_ge(sT.h, sT.n)
                    sS.inc(s.activation(xb2[c % 2][:, :], X[:, c, :],
                                        AF.Identity, bias=bia[:, c:c + 1],
                                        scale=scl[:, c:c + 1]))
                @blk.vector
                def _(v, c=c):
                    v.wait_ge(sS.h, sS.n)
                    if not l1:
                        sV.inc(v.tensor_tensor(xb2[c % 2][:, :], xb2[c % 2][:, :],
                                               normbc_v[:, :], op=ALU.mult))
                    else:
                        sV.inc(v.memset(fdum[0:1, 0:1], 0.0))
                @blk.tensor
                def _(t, c=c):
                    t.wait_ge(sV.h, sV.n)
                    ptb = ps1[:, 1024:1536].bitcast(BF16)
                    for j in range(8):
                        ins = t.transpose(ptb[:, j * 128:(j + 1) * 128],
                                          xb2[c % 2][:, j * 128:(j + 1) * 128],
                                          identb[:, :])
                    sT.inc(ins)
                @blk.vector
                def _(v, c=c):
                    v.wait_ge(sT.h, sT.n)
                    ptb = ps1[:, 1024:1536].bitcast(BF16)
                    sV.inc(v.tensor_copy(
                        at_xfm[:, :, :].rearrange("p a b -> p (a b)"), ptb[:, :]))
                for dj in range(8):
                    @blk.tensor
                    def _(t, c=c, dj=dj):
                        t.wait_ge(sV.h, sV.n)
                        t.wait_ge(sS.h, sS.n)
                        psK = ps0[:, 512 * (dj % 2):512 * (dj % 2) + 128]
                        for kj in range(8):
                            ins = t.matmul(psK,
                                           w_a[:, kj, dj * 128:(dj + 1) * 128],
                                           at_xfm[:, kj, :],
                                           start=(kj == 0), stop=(kj == 7))
                        sT.inc(ins)
                    @blk.scalar
                    def _(s, c=c, dj=dj):
                        s.wait_ge(sT.h, sT.n)
                        psK = ps0[:, 512 * (dj % 2):512 * (dj % 2) + 128]
                        sS.inc(s.activation(K_fm[:, dj, c * 128:(c + 1) * 128],
                                            psK, AF.Identity,
                                            bias=bfms[:, ksc_i, dj:dj + 1]))
                @blk.tensor
                def _(t, c=c):
                    t.wait_ge(sV.h, sV.n)
                    psV = ps0[:, 1024:2048]
                    for kj in range(8):
                        for n in range(2):
                            t.matmul(psV[:, n * 512:(n + 1) * 512],
                                     at_xfm[:, kj, :],
                                     w_b[:, kj, n * 512:(n + 1) * 512],
                                     start=(kj == 0), stop=False)
                    for n in range(2):
                        ins = t.matmul(psV[:, n * 512:(n + 1) * 512],
                                       onesb[0:1, :],
                                       browb_r[vrow][:, n * 512:(n + 1) * 512],
                                       start=False, stop=True)
                    sT.inc(ins)
                @blk.vector
                def _(v, c=c):
                    v.wait_ge(sT.h, sT.n)
                    sV.inc(v.tensor_copy(V_tm[:, c, :], ps0[:, 1024:2048]))
        load_weight_pair(wt["q"], w_c, wt["o"], w_d, with_rows=False, l1=l1)
        nheads = 1 if l1 else 16
        hd = 1024 if l1 else 64
        with nc.Block() as blk:
            qev = {0: sS.n, 1: sS.n}
            def prep_steps(cc):
                steps = []
                ps = {}
                def p_apply():
                    @blk.scalar
                    def _(s):
                        s.wait_ge(sU.h, sU.n)
                        sS.inc(s.activation(xb2[cc % 2][:, :], X[:, cc, :],
                                            AF.Identity, bias=bia[:, cc:cc + 1],
                                            scale=scl[:, cc:cc + 1]))
                    ps["apply"] = sS.n
                steps.append(p_apply)
                def p_norm():
                    @blk.vector
                    def _(v):
                        v.wait_ge(sS.h, ps["apply"])
                        if not l1:
                            sV.inc(v.tensor_tensor(xb2[cc % 2][:, :],
                                                   xb2[cc % 2][:, :],
                                                   normbc_v[:, :], op=ALU.mult))
                        else:
                            sV.inc(v.memset(fdum[0:1, 0:1], 0.0))
                    ps["norm"] = sV.n
                steps.append(p_norm)
                def p_trans():
                    @blk.tensor
                    def _(t):
                        t.wait_ge(sV.h, ps["norm"])
                        t.wait_ge(sS.h, max(qev[0], qev[1]))
                        ptb = ps1[:, 1024:1536].bitcast(BF16)
                        for j in range(8):
                            ins = t.transpose(ptb[:, j * 128:(j + 1) * 128],
                                              xb2[cc % 2][:, j * 128:(j + 1) * 128],
                                              identb[:, :])
                        sT.inc(ins)
                    ps["trans"] = sT.n
                steps.append(p_trans)
                def p_evac():
                    @blk.vector
                    def _(v):
                        v.wait_ge(sT.h, ps["trans"])
                        ptb = ps1[:, 1024:1536].bitcast(BF16)
                        sV.inc(v.tensor_copy(
                            at_xfm[:, :, :].rearrange("p a b -> p (a b)"),
                            ptb[:, :]))
                    ps["evac"] = sV.n
                steps.append(p_evac)
                for dj in range(8):
                    def p_q(dj=dj):
                        @blk.tensor
                        def _(t):
                            t.wait_ge(sV.h, ps["evac"])
                            t.wait_ge(sS.h, max(qev[0], qev[1]))
                            psQ = ps1[:, 1024 + 256 * (dj % 2):
                                      1152 + 256 * (dj % 2)]
                            for kj in range(8):
                                ins = t.matmul(psQ,
                                               w_c[:, kj, dj * 128:(dj + 1) * 128],
                                               at_xfm[:, kj, :],
                                               start=(kj == 0), stop=(kj == 7))
                            sT.inc(ins)
                        qmm = sT.n
                        @blk.scalar
                        def _(s):
                            s.wait_ge(sT.h, qmm)
                            psQ = ps1[:, 1024 + 256 * (dj % 2):
                                      1152 + 256 * (dj % 2)]
                            sS.inc(s.activation(q2[cc % 2][:, dj, :], psQ,
                                                AF.Identity, scale=qscale,
                                                bias=bfms[:, qsc_i, dj:dj + 1]))
                        qev[dj % 2] = sS.n
                    steps.append(p_q)
                return steps

            pending = prep_steps(0)
            for st in pending:
                st()
            exp_par = {0: sS.n, 1: sS.n}
            for c in range(NC):
                q_ready = sS.n
                pending = prep_steps(c + 1) if c + 1 < NC else []
                split = (not l1) and c < 8
                kl = (c + 1) * 128
                if split:
                    exp_cnt = {}
                    ev_cnt = {}
                    ev_seed = sV.n

                    def em_scores(h):
                        par = h % 2
                        base = 1024 * par
                        part = 64 * (h % 2)
                        dj0 = h // 2
                        @blk.tensor
                        def _(t, c=c, h=h, part=part, dj0=dj0, base=base,
                              w1=max(q_ready, exp_par[par])):
                            t.wait_ge(sS.h, w1)
                            t.wait_ge(sU.h, sU.n)
                            for a in range(0, kl, 512):
                                b = min(a + 512, kl)
                                ins = t.matmul(ps0[:, base + a:base + b],
                                               q2[c % 2][part:part + 64, dj0, :],
                                               K_fm[part:part + 64, dj0, a:b],
                                               start=True, stop=True)
                            sT.inc(ins)
                        @blk.vector
                        def _(v, base=base, par=par):
                            v.wait_ge(sT.h, sT.n)
                            v.tensor_tensor(ps0[:, base + kl - 128:base + kl],
                                            ps0[:, base + kl - 128:base + kl],
                                            causal[:, :], op=ALU.add)
                            sW.inc(v.memset(fdum[0:1, 0:1], 0.0))
                            v.wait_ge(sW.h, sW.n)
                            sV.inc(v.tensor_reduce(nmax[:, par:par + 1],
                                                   ps0[:, base:base + kl],
                                                   axis=AX.X, op=ALU.max,
                                                   negate=True))

                    def em_exp(h):
                        par = h % 2
                        base = 1024 * par
                        @blk.scalar
                        def _(s, h=h, base=base, par=par):
                            s.wait_ge(sV.h, sV.n)
                            sS.inc(s.activation(P_sb[:, base:base + kl],
                                                ps0[:, base:base + kl], AF.Exp,
                                                bias=nmax[:, par:par + 1],
                                                accum_out=rs16[:, h:h + 1]))
                        exp_cnt[h] = sS.n
                        exp_par[par] = sS.n

                    def em_ptpv(h):
                        par = h % 2
                        base = 1024 * par
                        @blk.tensor
                        def _(t, h=h, base=base, we=exp_cnt[h],
                              wv=ev_cnt.get(h - 2, ev_seed)):
                            t.wait_ge(sS.h, we)
                            t.wait_ge(sV.h, wv)
                            psPT = ps1[:, 1536:2048].bitcast(BF16)
                            for j in range(c + 1):
                                ins = t.transpose(
                                    psPT[:, j * 128:(j + 1) * 128],
                                    P_sb[:, base + j * 128:base + (j + 1) * 128],
                                    identb[:, :])
                            sT.inc(ins)
                        @blk.vector
                        def _(v, base=base):
                            v.wait_ge(sT.h, sT.n)
                            psPT = ps1[:, 1536:2048].bitcast(BF16)
                            sV.inc(v.tensor_copy(
                                PT_sb[:, base:base + kl],
                                psPT[:, 0:kl]))
                        ev_cnt[h] = sV.n
                        @blk.tensor
                        def _(t, c=c, h=h, base=base, wv=ev_cnt[h]):
                            t.wait_ge(sV.h, wv)
                            for j in range(c + 1):
                                ins = t.matmul(ps1[:, h * 64:(h + 1) * 64],
                                               PT_sb[:, base + j * 128:
                                                     base + (j + 1) * 128],
                                               V_tm[:, j, h * 64:(h + 1) * 64],
                                               start=(j == 0), stop=(j == c))
                            sT.inc(ins)

                    for h in range(nheads):
                        em_scores(h)
                        em_exp(h)
                        if h > 0:
                            em_ptpv(h - 1)
                        if pending:
                            pending.pop(0)()
                    em_ptpv(nheads - 1)
                else:
                  for h in range(nheads):
                    part = 0 if l1 else 64 * (h % 2)
                    dj0 = 0 if l1 else h // 2
                    prev_exp = max(exp_par[0], exp_par[1])
                    @blk.tensor
                    def _(t, c=c, h=h, part=part, dj0=dj0, kl=kl,
                          w1=max(q_ready, prev_exp)):
                        t.wait_ge(sS.h, w1)
                        t.wait_ge(sU.h, sU.n)
                        for a in range(0, kl, 512):
                            b = min(a + 512, kl)
                            if l1:
                                for kj in range(8):
                                    ins = t.matmul(ps0[:, a:b],
                                                   q2[c % 2][:, kj, :],
                                                   K_fm[:, kj, a:b],
                                                   start=(kj == 0),
                                                   stop=(kj == 7))
                            else:
                                ins = t.matmul(
                                    ps0[:, a:b],
                                    q2[c % 2][part:part + 64, dj0, :],
                                    K_fm[part:part + 64, dj0, a:b],
                                    start=True, stop=True)
                        sT.inc(ins)
                    @blk.vector
                    def _(v, c=c, kl=kl):
                        v.wait_ge(sT.h, sT.n)
                        v.tensor_tensor(ps0[:, kl - 128:kl], ps0[:, kl - 128:kl],
                                        causal[:, :], op=ALU.add)
                        sW.inc(v.memset(fdum[0:1, 0:1], 0.0))
                        v.wait_ge(sW.h, sW.n)
                        sV.inc(v.tensor_reduce(nmax[:, 0:1], ps0[:, 0:kl],
                                               axis=AX.X, op=ALU.max,
                                               negate=True))
                    @blk.scalar
                    def _(s, c=c, h=h, kl=kl):
                        s.wait_ge(sV.h, sV.n)
                        sS.inc(s.activation(P_sb[:, 0:kl], ps0[:, 0:kl], AF.Exp,
                                            bias=nmax[:, 0:1],
                                            accum_out=rs16[:, h:h + 1]))
                    exp_par[0] = sS.n
                    for ggi in range((c + 8) // 8):
                        j0, j1 = ggi * 8, min(ggi * 8 + 8, c + 1)
                        @blk.tensor
                        def _(t, j0=j0, j1=j1, we=exp_par[0]):
                            t.wait_ge(sS.h, we)
                            t.wait_ge(sV.h, sV.n)
                            psPT = ps1[:, 1536:2048].bitcast(BF16)
                            for j in range(j0, j1):
                                ins = t.transpose(
                                    psPT[:, (j - j0) * 128:(j - j0 + 1) * 128],
                                    P_sb[:, j * 128:(j + 1) * 128],
                                    identb[:, :])
                            sT.inc(ins)
                        @blk.vector
                        def _(v, j0=j0, j1=j1):
                            v.wait_ge(sT.h, sT.n)
                            psPT = ps1[:, 1536:2048].bitcast(BF16)
                            sV.inc(v.tensor_copy(
                                PT_sb[:, j0 * 128:j1 * 128],
                                psPT[:, 0:(j1 - j0) * 128]))
                    @blk.tensor
                    def _(t, c=c, h=h):
                        t.wait_ge(sV.h, sV.n)
                        if l1:
                            for n in range(2):
                                for j in range(c + 1):
                                    ins = t.matmul(
                                        ps1[:, n * 512:(n + 1) * 512],
                                        PT_sb[:, j * 128:(j + 1) * 128],
                                        V_tm[:, j, n * 512:(n + 1) * 512],
                                        start=(j == 0), stop=(j == c))
                        else:
                            for j in range(c + 1):
                                ins = t.matmul(ps1[:, h * 64:(h + 1) * 64],
                                               PT_sb[:, j * 128:(j + 1) * 128],
                                               V_tm[:, j, h * 64:(h + 1) * 64],
                                               start=(j == 0), stop=(j == c))
                        sT.inc(ins)
                    if pending and not l1:
                        pending.pop(0)()
                @blk.vector
                def _(v, c=c, nheads=nheads, hd=hd):
                    v.wait_ge(sT.h, sT.n)
                    v.wait_ge(sS.h, sS.n)
                    v.reciprocal(rc16[:, 0:nheads], rs16[:, 0:nheads])
                    sW.inc(v.memset(fdum[0:1, 0:1], 0.0))
                    v.wait_ge(sW.h, sW.n)
                    for h in range(nheads):
                        ins = v.tensor_scalar(at_o[:, h * hd:(h + 1) * hd],
                                              ps1[:, h * hd:(h + 1) * hd],
                                              rc16[:, h:h + 1], None,
                                              op0=ALU.mult)
                    sV.inc(ins)
                @blk.tensor
                def _(t, c=c):
                    t.wait_ge(sV.h, sV.n)
                    t.wait_ge(sS.h, sS.n)
                    ptb = ps1[:, 1024:1536].bitcast(BF16)
                    for j in range(8):
                        ins = t.transpose(ptb[:, j * 128:(j + 1) * 128],
                                          at_o[:, j * 128:(j + 1) * 128],
                                          identb[:, :])
                    sT.inc(ins)
                @blk.vector
                def _(v):
                    v.wait_ge(sT.h, sT.n)
                    ptb = ps1[:, 1024:1536].bitcast(BF16)
                    sV.inc(v.tensor_copy(
                        at_ofm[:, :, :].rearrange("p a b -> p (a b)"), ptb[:, :]))
                @blk.tensor
                def _(t, c=c, l1=l1):
                    t.wait_ge(sV.h, sV.n)
                    psY = ps0[:, 0:1024]
                    for j in range(8):
                        for n in range(2):
                            ins = t.matmul(psY[:, n * 512:(n + 1) * 512],
                                           at_ofm[:, j, :],
                                           w_d[:, j, n * 512:(n + 1) * 512],
                                           start=(j == 0),
                                           stop=(j == 7 and not l1))
                    if l1:
                        for n in range(2):
                            ins = t.matmul(psY[:, n * 512:(n + 1) * 512],
                                           onesb[0:1, :],
                                           browb_r[2][:, n * 512:(n + 1) * 512],
                                           start=False, stop=True)
                    sT.inc(ins)
                @blk.vector
                def _(v, c=c, l1=l1):
                    def vf(instr):
                        sW.inc(instr)
                        v.wait_ge(sW.h, sW.n)
                    v.wait_ge(sT.h, sT.n)
                    psY = ps0[:, 0:1024]
                    vf(v.tensor_tensor(X[:, c, :], X[:, c, :], psY[:, :],
                                       op=ALU.add))
                    if not l1:
                        vf(v.tensor_tensor(X[:, c, :], X[:, c, :],
                                           xb2[c % 2][:, :], op=ALU.add))
                        v.tensor_tensor(X[:, c, :], X[:, c, :], bobc_v[:, :],
                                        op=ALU.add)
                    sU.inc(v.memset(fdum[0:1, 1:2], 0.0))
                for st in pending:
                    st()
                pending = []

    wi_v = U2[:, :].bitcast(BF16).rearrange("p (a b) -> p a b", a=8)
    wo_v = U3[:, :].bitcast(BF16).rearrange("p (a b) -> p a b", a=16)
    m_v = U4[:, :].bitcast(BF16).rearrange("p (a b) -> p a b", a=8)
    pk_st = U5[:, 0:2048]
    gbuf = [U5[:, 0:3072], U5[:, 3072:6144],
            X[:, 8:11, :].rearrange("p a b -> p (a b)"),
            X[:, 11:14, :].rearrange("p a b -> p (a b)")]
    pk_xb = [U6[:, 0:512].bitcast(BF16), U6[:, 512:1024].bitcast(BF16)]
    pk_xfm = [U6[:, 1024 + 512 * i:1536 + 512 * i].bitcast(BF16)
              .rearrange("p (a b) -> p a b", a=8) for i in range(2)]
    pk_htm = U6[:, 2048:3072].bitcast(BF16)
    pk_hfm = U6[:, 3072:4096].bitcast(BF16).rearrange("p (a b) -> p a b", a=16)


    def pkm():
        ln_alpha(with_rms=False)
        with nc.Block() as blk:
            b_sV = sV.n
            b_dC1, b_dD1 = dC.n, dD.n
            @blk.sync
            def _(e):
                for j in range(8):
                    if j >= 1:
                        e.wait_ge(sV.h, b_sV + j)
                    dpar = dC if j % 2 == 0 else dD
                    dpar.inc(e.dma_start(pk_st[:, :],
                                         wint_d[j * 128:(j + 1) * 128, :]), 16)
            @blk.vector
            def _(v):
                for j in range(8):
                    bb = b_dC1 if j % 2 == 0 else b_dD1
                    v.wait_ge((dC if j % 2 == 0 else dD).h, bb + 16 * (j // 2 + 1))
                    sV.inc(v.tensor_copy(wi_v[:, j, :], pk_st[:, :]))
        with nc.Block() as blk:
            b_sV2 = sV.n
            b_dC2, b_dD2 = dC.n, dD.n
            @blk.sync
            def _(e):
                for j in range(16):
                    if j >= 1:
                        e.wait_ge(sV.h, b_sV2 + j)
                    dpar = dC if j % 2 == 0 else dD
                    dpar.inc(e.dma_start(pk_st[:, 0:1024],
                                         woutt_d[j * 128:(j + 1) * 128, :]), 16)
                for j in range(8):
                    dA.inc(e.dma_start(m_v[:, j, :],
                                       mdram[j * 128:(j + 1) * 128, :]), 16)
                e.wait_ge(dA.h, dA.n)
            @blk.vector
            def _(v):
                for j in range(16):
                    bb = b_dC2 if j % 2 == 0 else b_dD2
                    v.wait_ge((dC if j % 2 == 0 else dD).h, bb + 16 * (j // 2 + 1))
                    sV.inc(v.tensor_copy(wo_v[:, j, :], pk_st[:, 0:1024]))
        with nc.Block() as blk:
            swap_in = {}
            spill_a = {}
            xup_s = {}
            @blk.sync
            def _(e):
                for j in range(8):
                    dB.inc(e.dma_start(xsp1[j, :, :], X[:, 8 + j, :]), 16)
            spill_a["dB"] = dB.n
            gk = {}
            gsnap = {}
            stw = {}
            gelu_s = {}
            csnap = {}
            dsnap = {}
            tsnap = {}

            def emit_gather(k, c, h, table):
                par = c % 2
                dp = (dG, dH, dI, dJ)[k % 4]
                @blk.gpsimd
                def _(g):
                    g.wait_ge(dp.h, dp.n)
                    if k - 4 in gsnap:
                        vv, tt, ss, ww = gsnap[k - 4]
                        g.wait_ge(sV.h, vv)
                        g.wait_ge(sT.h, tt)
                        g.wait_ge(sS.h, ss)
                        g.wait_ge(sW.h, ww)
                    else:
                        g.wait_ge(sV.h, sV.n)
                    if k % 4 >= 2 and k < 4:
                        g.wait_ge(dB.h, spill_a["dB"])
                    dp.inc(g.dma_gather(
                        gbuf[k % 4].unsqueeze(1), table.ap(),
                        wrap16[:, par, 8 * h:8 * h + 8], 128, gnum, 3072), 16)
                gk[k] = dp.n

            def emit_in_dot(k, c, h, last):
                par = c % 2
                dp = (dG, dH, dI, dJ)[k % 4]
                gb = gbuf[k % 4]
                @blk.scalar
                def _(s):
                    s.wait_ge(dp.h, gk[k])
                    if k - 4 in tsnap:
                        s.wait_ge(sT.h, tsnap[k - 4])
                    if k - 4 in stw:
                        s.wait_ge(sW.h, stw[k - 4])
                    sS.inc(s.activation(gb[:, 0:1024].bitcast(BF16),
                                        gb[:, 0:2048], AF.Identity))
                csnap[k] = sS.n
                @blk.vector
                def _(v):
                    v.wait_ge(dp.h, gk[k])
                    if k - 1 in dsnap:
                        v.wait_ge(sV.h, dsnap[k - 1])
                    sW.inc(v.scalar_tensor_tensor(
                        gb[:, 2048:3072], gb[:, 2048:3072], gate8[:, par, h:h + 1],
                        pk_xb[par][:, :], op0=ALU.mult, op1=ALU.mult,
                        accum_out=lin1[:, :]))
                    v.wait_ge(sW.h, sW.n)
                    if k - 1 in tsnap:
                        v.wait_ge(sT.h, tsnap[k - 1])
                    sV.inc(v.tensor_scalar(diagt[:, 0:64].bitcast(BF16),
                                           i128f[:, :], lin1[:, 0:1], None,
                                           op0=ALU.mult))
                stw[k] = sW.n
                dsnap[k] = sV.n
                @blk.tensor
                def _(t):
                    t.wait_ge(sV.h, dsnap[k])
                    t.wait_ge(sS.h, csnap[k])
                    gb16 = gb[:, 0:1024].bitcast(BF16)
                    for a in range(0, 2048, 512):
                        ins = t.matmul(ps0[:, a:a + 512],
                                       diagt[:, 0:64].bitcast(BF16),
                                       gb16[:, a:a + 512],
                                       start=False, stop=last)
                    sT.inc(ins)
                tsnap[k] = sT.n
                gsnap[k] = (sV.n, sT.n, sS.n, sW.n)

            def emit_out_dot(k, c, h, last):
                par = c % 2
                dp = (dG, dH, dI, dJ)[k % 4]
                gb = gbuf[k % 4]
                @blk.scalar
                def _(s):
                    s.wait_ge(dp.h, gk[k])
                    if k - 4 in tsnap:
                        s.wait_ge(sT.h, tsnap[k - 4])
                    if k - 4 in stw:
                        s.wait_ge(sW.h, stw[k - 4])
                    sS.inc(s.activation(gb[:, 0:512].bitcast(BF16),
                                        gb[:, 0:1024], AF.Identity))
                csnap[k] = sS.n
                @blk.vector
                def _(v):
                    v.wait_ge(dp.h, gk[k])
                    if k - 1 in dsnap:
                        v.wait_ge(sV.h, dsnap[k - 1])
                    v.wait_ge(sS.h, gelu_s[c])
                    sW.inc(v.scalar_tensor_tensor(
                        gb[:, 1024:3072], gb[:, 1024:3072], gate8[:, par, h:h + 1],
                        pk_htm[:, :], op0=ALU.mult, op1=ALU.mult,
                        accum_out=lin1[:, :]))
                    v.wait_ge(sW.h, sW.n)
                    if k - 1 in tsnap:
                        v.wait_ge(sT.h, tsnap[k - 1])
                    sV.inc(v.tensor_scalar(diagt[:, 0:64].bitcast(BF16),
                                           i128f[:, :], lin1[:, 0:1], None,
                                           op0=ALU.mult))
                stw[k] = sW.n
                dsnap[k] = sV.n
                @blk.tensor
                def _(t):
                    t.wait_ge(sV.h, dsnap[k])
                    t.wait_ge(sS.h, csnap[k])
                    gb16 = gb[:, 0:512].bitcast(BF16)
                    for n in range(2):
                        ins = t.matmul(ps0[:, n * 512:(n + 1) * 512],
                                       diagt[:, 0:64].bitcast(BF16),
                                       gb16[:, n * 512:(n + 1) * 512],
                                       start=False, stop=last)
                    sT.inc(ins)
                tsnap[k] = sT.n
                gsnap[k] = (sV.n, sT.n, sS.n, sW.n)

            def emit_prelude(c):
                par = c % 2
                sl = c if c < 8 else c - 8
                @blk.scalar
                def _(s):
                    if c >= 8:
                        s.wait_ge(dA.h, swap_in[sl])
                    sS.inc(s.activation(pk_xb[par][:, :], X[:, sl, :],
                                        AF.Identity, bias=biasA[:, c:c + 1],
                                        scale=scaleA[:, c:c + 1]))
                @blk.tensor
                def _(t):
                    t.wait_ge(sS.h, sS.n)
                    t.wait_ge(sV.h, sV.n)
                    ptb = ps1[:, 1024:1536].bitcast(BF16)
                    for j in range(8):
                        ins = t.transpose(ptb[:, j * 128:(j + 1) * 128],
                                          pk_xb[par][:, j * 128:(j + 1) * 128],
                                          identb[:, :])
                    sT.inc(ins)
                @blk.vector
                def _(v):
                    v.wait_ge(sT.h, sT.n)
                    ptb = ps1[:, 1024:1536].bitcast(BF16)
                    sV.inc(v.tensor_copy(
                        pk_xfm[par][:, :, :].rearrange("p a b -> p (a b)"),
                        ptb[:, :]))
                @blk.tensor
                def _(t):
                    t.wait_ge(sV.h, sV.n)
                    for j in range(8):
                        for a in range(0, 2048, 512):
                            ins = t.matmul(ps1[:, a:a + 512],
                                           pk_xfm[par][:, j, :],
                                           m_v[:, j, a:a + 512],
                                           start=(j == 0), stop=(j == 7))
                    sT.inc(ins)
                sims3 = ps1[:, 0:2048].rearrange("p (a b) -> p a b", a=16)
                @blk.vector
                def _(v):
                    def vfw(instr):
                        sW.inc(instr)
                        v.wait_ge(sW.h, sW.n)
                    v.wait_ge(sT.h, sT.n)
                    vfw(v.tensor_reduce(m16[:, :], sims3, axis=AX.X, op=ALU.max))
                    mbc = m16[:, :].unsqueeze(2).broadcast_to([128, 16, 128])
                    vfw(v.tensor_tensor(sims3, sims3, mbc, op=ALU.is_equal))
                    ibc = iotaB[:, :].unsqueeze(1).broadcast_to([128, 16, 128])
                    vfw(v.scalar_tensor_tensor(sims3, sims3, -256.0, ibc,
                                               op0=ALU.mult, op1=ALU.add))
                    vfw(v.tensor_reduce(a16[:, :], sims3, axis=AX.X, op=ALU.min))
                    vfw(v.tensor_copy(a16b_v[:, :], a16[:, :]))
                    vfw(v.tensor_tensor(gate8[:, par, :], m16[:, 0:8],
                                        m16[:, 8:16], op=ALU.add))
                    sV.inc(v.tensor_scalar_max(gate8[:, par, :],
                                               gate8[:, par, :], 0.0))
                pw0 = ps1[:, 0:64].rearrange("p (a b) -> p a b", a=8)
                pw1 = ps1[:, 64:128].rearrange("p (a b) -> p a b", a=8)
                @blk.tensor
                def _(t):
                    t.wait_ge(sV.h, sV.n)
                    for r in range(8):
                        t.matmul(pw0[:, :, r], permrep[:, r, :], a16b_v[:, 0:8],
                                 start=True, stop=True)
                        ins = t.matmul(pw1[:, :, r], permrep[:, r, :],
                                       a16b_v[:, 8:16], start=True, stop=True)
                    sT.inc(ins)
                @blk.vector
                def _(v):
                    v.wait_ge(sT.h, sT.n)
                    sW.inc(v.tensor_copy(
                        wtmp_v[:, :], pw0[:, :, :].rearrange("p a b -> p (a b)")))
                    v.wait_ge(sW.h, sW.n)
                    sV.inc(v.scalar_tensor_tensor(
                        wrap16[:, par, :], wtmp_v[:, :], 128.0,
                        pw1[:, :, :].rearrange("p a b -> p (a b)"),
                        op0=ALU.mult, op1=ALU.add))

            def emit_win(c):
                par = c % 2
                @blk.tensor
                def _(t):
                    t.wait_ge(sU.h, sU.n)
                    t.wait_ge(sV.h, sV.n)
                    for j in range(8):
                        for a in range(0, 2048, 512):
                            ins = t.matmul(ps0[:, a:a + 512],
                                           pk_xfm[par][:, j, :],
                                           wi_v[:, j, a:a + 512],
                                           start=(j == 0), stop=False)
                    sT.inc(ins)

            state = {"next": 0}

            def pump(upto):
                while state["next"] <= min(upto, 16 * NC - 1):
                    k = state["next"]
                    kc, kh = k // 16, k % 16
                    if kh < 8:
                        emit_gather(k, kc, kh, tin_d)
                    else:
                        emit_gather(k, kc, kh - 8, tout_d)
                    state["next"] += 1

            emit_prelude(0)
            emit_win(0)
            pump(3)
            for c in range(NC):
                for h in range(8):
                    emit_in_dot(16 * c + h, c, h, last=(h == 7))
                    pump(16 * c + h + 4)
                if c + 1 < NC:
                    emit_prelude(c + 1)
                @blk.scalar
                def _(s, c=c):
                    s.wait_ge(sT.h, sT.n)
                    sS.inc(s.activation(pk_htm[:, :], ps0[:, 0:2048], AF.Gelu))
                gelu_s[c] = sS.n
                for gg in range(2):
                    @blk.tensor
                    def _(t, gg=gg):
                        t.wait_ge(sS.h, sS.n)
                        t.wait_ge(sV.h, sV.n)
                        ptb = ps0[:, 1536:2048].bitcast(BF16)
                        for j in range(8):
                            ins = t.transpose(
                                ptb[:, j * 128:(j + 1) * 128],
                                pk_htm[:, (gg * 8 + j) * 128:
                                       (gg * 8 + j + 1) * 128],
                                identb[:, :])
                        sT.inc(ins)
                    @blk.vector
                    def _(v, gg=gg):
                        v.wait_ge(sT.h, sT.n)
                        ptb = ps0[:, 1536:2048].bitcast(BF16)
                        sV.inc(v.tensor_copy(
                            pk_hfm[:, gg * 8:gg * 8 + 8, :].rearrange(
                                "p a b -> p (a b)"), ptb[:, :]))
                @blk.tensor
                def _(t):
                    t.wait_ge(sV.h, sV.n)
                    for j in range(16):
                        for n in range(2):
                            ins = t.matmul(ps0[:, n * 512:(n + 1) * 512],
                                           pk_hfm[:, j, :],
                                           wo_v[:, j, n * 512:(n + 1) * 512],
                                           start=(j == 0), stop=False)
                    sT.inc(ins)
                for h in range(8):
                    emit_out_dot(16 * c + 8 + h, c, h, last=(h == 7))
                    pump(16 * c + 8 + h + 4)
                sl = c if c < 8 else c - 8
                @blk.vector
                def _(v, c=c, sl=sl):
                    v.wait_ge(sT.h, sT.n)
                    sU.inc(v.scalar_tensor_tensor(
                        X[:, sl, :], ps0[:, 0:1024], 0.5, X[:, sl, :],
                        op0=ALU.mult, op1=ALU.add))
                xup_s[c] = sU.n
                @blk.sync
                def _(e, c=c, sl=sl):
                    if c < 8:
                        e.wait_ge(sU.h, xup_s[c])
                        dB.inc(e.dma_start(xsp2[sl, :, :], X[:, sl, :]), 16)
                        e.wait_ge(dB.h, dB.n)
                        e.wait_ge(dA.h, dA.n)
                        dA.inc(e.dma_start(X[:, sl, :], xsp1[sl, :, :]), 16)
                        swap_in[sl] = dA.n
                    else:
                        e.wait_ge(sU.h, xup_s[c])
                        dB.inc(e.dma_start(xsp1[sl, :, :], X[:, sl, :]), 16)
                if c + 1 < NC:
                    emit_win(c + 1)
            @blk.sync
            def _(e):
                for j in range(8):
                    e.wait_ge(dB.h, dB.n)
                for j in range(8):
                    dA.inc(e.dma_start(X[:, j, :], xsp2[j, :, :]), 16)
                e.wait_ge(sT.h, sT.n)
                e.wait_ge(sV.h, sV.n)
                e.wait_ge(sS.h, sS.n)
                for j in range(8):
                    dA.inc(e.dma_start(X[:, 8 + j, :], xsp1[j, :, :]), 16)
                e.wait_ge(dA.h, dA.n)

    def write_out():
        with nc.Block() as blk:
            @blk.sync
            def _(e):
                for c in range(NC):
                    dO.inc(e.dma_start(out_d[c * 128:(c + 1) * 128, :],
                                       X[:, c, :]), 16)
                e.wait_ge(dO.h, dO.n)

    def dump_dbg():
        if dbg_d is None:
            return
        with nc.Block() as blk:
            @blk.sync
            def _(e):
                dO.inc(e.dma_start(dbg_d[:, 0:32],
                                   mv[:, :, :].rearrange("p a b -> p (a b)")), 16)
                dO.inc(e.dma_start(dbg_d[:, 32:48], scaleA[:, :]), 16)
                dO.inc(e.dma_start(dbg_d[:, 48:64], biasA[:, :]), 16)
                e.wait_ge(dO.h, dO.n)

    if STOP >= 1:
        sign_ffn()
    dump_dbg()
    if STOP >= 2:
        attn(l1=False)
    if STOP >= 3:
        sign_ffn()
    if STOP >= 4:
        pkm()
    if STOP >= 5:
        attn(l1=True)
    if STOP >= 6:
        pkm()
    write_out()

    stack.close()
    return nc


def _host_inputs(inp):
    f32 = np.float32

    def tr(a):
        return np.ascontiguousarray(np.asarray(a, dtype=f32).T)

    def fm(b):
        return np.ascontiguousarray(np.asarray(b, dtype=f32).reshape(8, 128).T)

    def row(b):
        return np.ascontiguousarray(np.asarray(b, dtype=f32).reshape(1, D))

    com = {
        "w0t": tr(inp["L0_ffn_weight"]),
        "normw": row(inp["L0_attn_norm_w"]),
        "wqt0": tr(inp["L0_attn_Wq"]), "wkt0": tr(inp["L0_attn_Wk"]),
        "wvt0": tr(inp["L0_attn_Wv"]), "wot0": tr(inp["L0_attn_Wo"]),
        "bq0f": fm(inp["L0_attn_bq"]), "bk0f": fm(inp["L0_attn_bk"]),
        "bv0r": row(inp["L0_attn_bv"]), "bo0r": row(inp["L0_attn_bo"]),
        "wqt1": tr(inp["L1_attn_w_q"]), "wkt1": tr(inp["L1_attn_w_k"]),
        "wvt1": tr(inp["L1_attn_w_v"]), "wot1": tr(inp["L1_attn_w_o"]),
        "bq1f": fm(inp["L1_attn_b_q"]), "bk1f": fm(inp["L1_attn_b_k"]),
        "bv1r": row(inp["L1_attn_b_v"]), "bo1r": row(inp["L1_attn_b_o"]),
        "wq8k": np.ascontiguousarray(np.asarray(inp["L1_ffn_Wq"], dtype=f32)),
        "keyst": np.ascontiguousarray(
            np.asarray(inp["L1_ffn_keys"], dtype=f32).transpose(2, 0, 3, 1)),
        "wint": tr(inp["L1_ffn_W_in"]),
        "woutt": tr(inp["L1_ffn_W_out"]),
        "tin": np.ascontiguousarray(np.concatenate(
            [np.asarray(inp["L1_ffn_in_b"], dtype=f32),
             np.asarray(inp["L1_ffn_in_a"], dtype=f32)], axis=1)),
        "tout": np.ascontiguousarray(np.concatenate(
            [np.asarray(inp["L1_ffn_out_b"], dtype=f32),
             np.asarray(inp["L1_ffn_out_a"], dtype=f32)], axis=1)),
    }
    xs = np.ascontiguousarray(np.asarray(inp["L0_attn_x"], dtype=f32))
    return com, xs


def kernel(**inp):
    com, xs = _host_inputs(inp)
    nc = _build()
    import bass_rust as _br
    _br.insert_act_table_loads(nc)
    lower_extended_insts(nc)
    in_maps = [dict(com, x=np.ascontiguousarray(xs[c])) for c in range(8)]
    res = run_bass_kernel_spmd(nc, in_maps, core_ids=list(range(8)))
    return np.stack([np.asarray(res.results[c]["out"]) for c in range(8)], axis=0)


# version 16
# speedup vs baseline: 1.0599x; 1.0599x over previous
import os
from contextlib import ExitStack

import numpy as np
import concourse.bass as bass
import concourse.mybir as mybir
from concourse.bass_utils import run_bass_kernel_spmd
from concourse import library_config
from concourse.library_overlay import lower_extended_insts

F32 = mybir.dt.float32
F32R = mybir.dt.float32r
BF16 = mybir.dt.bfloat16
I16 = mybir.dt.int16
AF = mybir.ActivationFunctionType
ALU = mybir.AluOpType
AX = mybir.AxisListType

T, D = 2048, 1024
NC = 16
D2 = 2048
STOP = int(os.environ.get("MK_STOP", "6"))


class Sem:
    def __init__(self, nc, stack, name):
        self.h = stack.enter_context(nc.semaphore(name))
        self.n = 0

    def inc(self, instr, k=1):
        instr.then_inc(self.h, k)
        self.n += k


def _build():
    nc = bass.Bass("TRN2", target_bir_lowering=False)

    def din(name, shape, dtype=F32):
        return nc.dram_tensor(name, list(shape), dtype, kind="ExternalInput")

    x_d = din("x", [T, D])
    w0t_d = din("w0t", [D, D])
    normw_d = din("normw", [1, D])
    wt0_d = {k: din("w%st0" % k, [D, D]) for k in "qkvo"}
    wt1_d = {k: din("w%st1" % k, [D, D]) for k in "qkvo"}
    bq0f_d = din("bq0f", [128, 8]); bk0f_d = din("bk0f", [128, 8])
    bq1f_d = din("bq1f", [128, 8]); bk1f_d = din("bk1f", [128, 8])
    bv0r_d = din("bv0r", [1, D]); bo0r_d = din("bo0r", [1, D])
    bv1r_d = din("bv1r", [1, D]); bo1r_d = din("bo1r", [1, D])
    wq8k_d = din("wq8k", [8192, D])
    keyst_d = din("keyst", [2, 8, 512, 128])
    wint_d = din("wint", [D, D2])
    woutt_d = din("woutt", [D2, D])
    tin_d = din("tin", [16384, 3072])
    tout_d = din("tout", [16384, 3072])

    out_d = nc.dram_tensor("out", [T, D], F32, kind="ExternalOutput")
    dbg_d = (nc.dram_tensor("dbg", [128, 64], F32, kind="ExternalOutput")
             if os.environ.get("MK_DBG") else None)
    mdram = nc.dram_tensor("mdram", [1024, 2048], BF16, kind="Internal")
    xsp1 = nc.dram_tensor("xsp1", [8, 128, 1024], F32, kind="Internal")
    xsp2 = nc.dram_tensor("xsp2", [8, 128, 1024], F32, kind="Internal")

    stack = ExitStack()

    def sbuf(name, shape, dtype):
        return stack.enter_context(nc.sbuf_tensor(name, list(shape), dtype))

    def psum(name, shape, dtype):
        return stack.enter_context(nc.psum_tensor(name, list(shape), dtype))

    X = sbuf("X", [128, NC, D], F32)
    U2 = sbuf("U2", [128, 8192], F32)
    U3 = sbuf("U3", [128, 8192], F32)
    U4 = sbuf("U4", [128, 8192], F32)
    U5 = sbuf("U5", [128, 6144], F32)
    U6 = sbuf("U6", [128, 4096], F32)
    identb = sbuf("identb", [128, 128], BF16)
    i128f = sbuf("i128f", [128, 128], F32)
    causal = sbuf("causal", [128, 128], F32)
    iotaB = sbuf("iotaB", [128, 128], F32)
    permrep = sbuf("permrep", [128, 8, 128], BF16)
    onesb = sbuf("onesb", [1, 128], BF16)
    bfm = sbuf("bfm", [128, 4, 8], F32)
    bfms = sbuf("bfms", [128, 4, 8], F32)
    mv = sbuf("mv", [128, NC, 2], F32)
    stat6 = sbuf("stat6", [128, 2, 6], F32)
    stdt = sbuf("stdt", [128, NC], F32)
    scaleA = sbuf("scaleA", [128, NC], F32)
    biasA = sbuf("biasA", [128, NC], F32)
    scaleB = sbuf("scaleB", [128, NC], F32)
    biasB = sbuf("biasB", [128, NC], F32)
    tmpc = sbuf("tmpc", [128, NC], F32)
    lin1 = sbuf("lin1", [128, 1], F32)
    gate8 = sbuf("gate8", [128, 2, 8], F32)
    m16 = sbuf("m16", [128, 16], F32)
    a16 = sbuf("a16", [128, 16], F32)
    idxf = sbuf("idxf", [128, 8], F32)
    wrap16 = sbuf("wrap16", [128, 2, 64], I16)
    a16b_v = sbuf("a16b", [128, 16], BF16)
    wtmp_v = sbuf("wtmp", [128, 64], F32)
    diagt = sbuf("diagt", [128, 128], F32)
    rs16 = sbuf("rs16", [128, 16], F32)
    rc16 = sbuf("rc16", [128, 16], F32)
    nmax = sbuf("nmax", [128, 2], F32)
    fdum = sbuf("fdum", [1, 2], F32)

    gnum = stack.enter_context(nc.gpsimd.register("gnum"))
    ps0 = psum("ps0", [128, 2048], F32)
    ps1 = psum("ps1", [128, 2048], F32)

    sems = {nm: Sem(nc, stack, "s_" + nm)
            for nm in ["dA", "dB", "dC", "dD", "dG", "dH", "dI", "dJ", "dO",
                       "sS", "sV", "sW", "sT", "sU", "sQ"]}
    dA, dB, dG, dO = sems["dA"], sems["dB"], sems["dG"], sems["dO"]
    dC, dD, dH = sems["dC"], sems["dD"], sems["dH"]
    dI, dJ = sems["dI"], sems["dJ"]
    sS, sV, sW, sT, sU, sQ = (sems["sS"], sems["sV"], sems["sW"],
                              sems["sT"], sems["sU"], sems["sQ"])

    with nc.Block() as blk:
        @blk.gpsimd
        def _(g):
            def fence(instr):
                sQ.inc(instr)
                g.wait_ge(sQ.h, sQ.n)
            for tile in (identb, i128f):
                fence(g.memset(tile[:, :], 1.0))
                g.affine_select(out=tile[:, :], in_=tile[:, :],
                                compare_op=ALU.is_equal, fill=0.0, base=0,
                                pattern=[[-1, 128]], channel_multiplier=1)
            fence(g.memset(causal[:, :], 0.0))
            g.affine_select(out=causal[:, :], in_=causal[:, :],
                            compare_op=ALU.is_ge, fill=-1e30, base=0,
                            pattern=[[-1, 128]], channel_multiplier=1)
            g.iota(iotaB[:, :], [[1, 128]], base=256, channel_multiplier=0,
                   allow_small_or_imprecise_dtypes=True)
            for r in range(8):
                sl = permrep[:, r, :]
                fence(g.memset(sl, 1.0))
                g.affine_select(out=sl, in_=sl, compare_op=ALU.is_equal,
                                fill=0.0, base=-16 * r,
                                pattern=[[0, 8], [-1, 16]],
                                channel_multiplier=1)
            g.memset(onesb[:, :], 1.0)
            g.reg_mov(gnum, 128)
            g.load_library(library_config.mlp)

    with nc.Block() as blk:
        @blk.sync
        def _(e):
            for dst, src in [
                (bfm[:, 0, :], bq0f_d.ap()), (bfm[:, 1, :], bk0f_d.ap()),
                (bfm[:, 2, :], bq1f_d.ap()), (bfm[:, 3, :], bk1f_d.ap()),
            ]:
                dA.inc(e.dma_start(dst, src), 16)
            for c in range(NC):
                dA.inc(e.dma_start(X[:, c, :], x_d[c * 128:(c + 1) * 128, :]), 16)
            e.wait_ge(dA.h, dA.n)
        @blk.vector
        def _(v):
            v.wait_ge(dA.h, dA.n)
            v.tensor_scalar_mul(bfms[:, 0, :], bfm[:, 0, :], 0.125)
            v.tensor_scalar_mul(bfms[:, 1, :], bfm[:, 1, :], 1.0)
            v.tensor_scalar_mul(bfms[:, 2, :], bfm[:, 2, :], 1024.0 ** -0.5)
            v.tensor_scalar_mul(bfms[:, 3, :], bfm[:, 3, :], 1.0)

    wq_st = U5[:, 0:4096].rearrange("p (a b) -> p a b", a=4)
    ky_st = U5[:, 4096:4608].rearrange("p (a b) -> p a b", a=4)
    wq_b = U6[:, 0:2048].bitcast(BF16).rearrange("p (a b) -> p a b", a=4)
    ky_b = U6[:, 2048:2304].bitcast(BF16).rearrange("p (a b) -> p a b", a=4)
    m_ev = U6[:, 2304:2432].bitcast(BF16).rearrange("p (a b) -> p a b", a=2)
    with nc.Block() as blk:
        b_dB, b_sV, b_sT, b_sU = dB.n, sV.n, sT.n, sU.n
        b_dG, b_dH = dG.n, dH.n

        @blk.sync
        def _(e):
            for ph in range(16):
                if ph >= 1:
                    e.wait_ge(sV.h, b_sV + ph)
                for kd in range(4):
                    dB.inc(e.dma_start(
                        wq_st[:, kd, :],
                        wq8k_d[ph * 512 + kd * 128: ph * 512 + (kd + 1) * 128, :]), 16)
                    dB.inc(e.dma_start(
                        ky_st[:, kd, :],
                        keyst_d[ph // 8, ph % 8, kd * 128:(kd + 1) * 128, :]), 16)
        @blk.vector
        def _(v):
            for ph in range(16):
                v.wait_ge(dB.h, b_dB + 128 * (ph + 1))
                v.tensor_copy(wq_b[:, :, :], wq_st[:, :, :])
                sV.inc(v.tensor_copy(ky_b[:, :, :], ky_st[:, :, :]))
                for m in range(8):
                    k = ph * 8 + m
                    v.wait_ge(sT.h, b_sT + k + 1)
                    if k >= 2:
                        dpar = dG if k % 2 == 0 else dH
                        v.wait_ge(dpar.h, (b_dG if k % 2 == 0 else b_dH)
                                  + 16 * (k // 2))
                    sU.inc(v.tensor_copy(
                        m_ev[:, m % 2, :],
                        ps0[:, 512 * (m % 2):512 * (m % 2) + 128]))
        @blk.tensor
        def _(t):
            for ph in range(16):
                t.wait_ge(sV.h, b_sV + ph + 1)
                for m in range(8):
                    if ph * 8 + m >= 2:
                        t.wait_ge(sU.h, b_sU + ph * 8 + m - 1)
                    pm = ps0[:, 512 * (m % 2):512 * (m % 2) + 128]
                    for kd in range(4):
                        ins = t.matmul(pm, wq_b[:, kd, m * 128:(m + 1) * 128],
                                       ky_b[:, kd, :],
                                       start=(kd == 0), stop=(kd == 3))
                    sT.inc(ins)
        @blk.gpsimd
        def _(g):
            for ph in range(16):
                for m in range(8):
                    g.wait_ge(sU.h, b_sU + ph * 8 + m + 1)
                    dpar = dG if m % 2 == 0 else dH
                    dpar.inc(g.dma_start(
                        mdram[m * 128:(m + 1) * 128, ph * 128:(ph + 1) * 128],
                        m_ev[:, m % 2, :]), 16)
            g.wait_ge(dG.h, dG.n)
            g.wait_ge(dH.h, dH.n)

    def ln_alpha(with_rms=False):
        with nc.Block() as blk:
            b_sV, b_sS = sV.n, sS.n

            @blk.vector
            def _(v):
                def vf(instr):
                    sW.inc(instr)
                    v.wait_ge(sW.h, sW.n)
                for c in range(NC):
                    xc = X[:, c, :]
                    v.wait_ge(sV.h, b_sV + c)
                    v.bn_stats(stat6[:, 0, :], xc[:, 0:512])
                    vf(v.bn_stats(stat6[:, 1, :], xc[:, 512:1024]))
                    sV.inc(v.bn_aggr(mv[:, c, :], stat6[:, :, :]))
                for c in range(NC):
                    v.wait_ge(sS.h, b_sS + c + 1)
                    vf(v.reciprocal(scaleA[:, c:c + 1], stdt[:, c:c + 1]))
                    v.tensor_scalar(biasA[:, c:c + 1], mv[:, c, 0:1],
                                    scaleA[:, c:c + 1], -1.0,
                                    op0=ALU.mult, op1=ALU.mult)
                if with_rms:
                    for c in range(NC):
                        v.tensor_scalar(tmpc[:, c:c + 1], mv[:, c, 1:2],
                                        scaleA[:, c:c + 1], scaleA[:, c:c + 1],
                                        op0=ALU.mult, op1=ALU.mult)
                    sV.inc(v.memset(fdum[0:1, 0:1], 0.0))
                    for c in range(NC):
                        v.wait_ge(sS.h, b_sS + NC + c + 1)
                        vf(v.reciprocal(tmpc[:, c:c + 1], stdt[:, c:c + 1]))
                    sV.inc(v.tensor_tensor(scaleB[:, :], scaleA[:, :], tmpc[:, :],
                                           op=ALU.mult))
                    v.tensor_tensor(biasB[:, :], biasA[:, :], tmpc[:, :],
                                    op=ALU.mult)
            @blk.scalar
            def _(s):
                for c in range(NC):
                    s.wait_ge(sV.h, b_sV + c + 1)
                    sS.inc(s.activation(stdt[:, c:c + 1], mv[:, c, 1:2],
                                        AF.Sqrt, bias=epsln_t[:, :]))
                if with_rms:
                    s.wait_ge(sV.h, b_sV + NC + 1)
                    for c in range(NC):
                        sS.inc(s.activation(stdt[:, c:c + 1], tmpc[:, c:c + 1],
                                            AF.Sqrt, bias=epsrm_t[:, :]))
    epsln_t = sbuf("epsln", [128, 1], F32)
    epsrm_t = sbuf("epsrm", [128, 1], F32)

    with nc.Block() as blk:
        @blk.vector
        def _(v):
            v.memset(epsln_t[:, :], 1e-5)
            v.memset(epsrm_t[:, :], 1e-6)

    sW0 = U4[:, 0:4096].bitcast(BF16).rearrange("p (a b) -> p a b", a=8)
    w0st = U5[:, 0:2048].rearrange("p (a b) -> p a b", a=2)

    def sign_ffn():
        ln_alpha(False)
        with nc.Block() as blk:
            b_dC, b_dD, b_sS = dC.n, dD.n, sS.n

            @blk.sync
            def _(e):
                for j in range(8):
                    if j >= 2:
                        e.wait_ge(sS.h, b_sS + j - 1)
                    dpar = dC if j % 2 == 0 else dD
                    dpar.inc(e.dma_start(w0st[:, j % 2, :],
                                         w0t_d[j * 128:(j + 1) * 128, :]), 16)
            @blk.scalar
            def _(s):
                for j in range(8):
                    s.wait_ge((dC if j % 2 == 0 else dD).h,
                              (b_dC if j % 2 == 0 else b_dD) + 16 * (j // 2 + 1))
                    sS.inc(s.activation(sW0[:, j, :], w0st[:, j % 2, :], AF.Sign))
        xnb = U6[:, 0:1024].bitcast(BF16).rearrange("p (a b) -> p a b", a=2)
        xnfm = U6[:, 1024:2048].bitcast(BF16).rearrange("p (a b c) -> p a b c",
                                                        a=2, b=8)
        with nc.Block() as blk:
            b_sS, b_sT, b_sV, b_sU, b_sW = sS.n, sT.n, sV.n, sU.n, sW.n

            @blk.scalar
            def _(s):
                for c in range(NC):
                    if c >= 2:
                        s.wait_ge(sT.h, b_sT + 2 * (c - 1))
                    sS.inc(s.activation(xnb[:, c % 2, :], X[:, c, :], AF.Identity,
                                        bias=biasA[:, c:c + 1],
                                        scale=scaleA[:, c:c + 1]))
            @blk.tensor
            def _(t):
                for c in range(NC):
                    t.wait_ge(sS.h, b_sS + c + 1)
                    if c >= 2:
                        t.wait_ge(sV.h, b_sV + c - 1)
                    ptb = ps1[:, 512 * (c % 2):512 * (c % 2) + 512].bitcast(BF16)
                    for j in range(8):
                        ins = t.transpose(ptb[:, j * 128:(j + 1) * 128],
                                          xnb[:, c % 2, j * 128:(j + 1) * 128],
                                          identb[:, :])
                    sT.inc(ins)
                    t.wait_ge(sV.h, b_sV + c + 1)
                    if c >= 2:
                        t.wait_ge(sU.h, b_sU + c - 1)
                    py = ps0[:, 1024 * (c % 2):1024 * (c % 2) + 1024]
                    for j in range(8):
                        for n in range(2):
                            ins = t.matmul(py[:, n * 512:(n + 1) * 512],
                                           xnfm[:, c % 2, j, :],
                                           sW0[:, j, n * 512:(n + 1) * 512],
                                           start=(j == 0), stop=(j == 7))
                    sT.inc(ins)
            @blk.vector
            def _(v):
                for c in range(NC):
                    v.wait_ge(sT.h, b_sT + 2 * c + 1)
                    ptb = ps1[:, 512 * (c % 2):512 * (c % 2) + 512].bitcast(BF16)
                    sV.inc(v.tensor_copy(
                        xnfm[:, c % 2, :, :].rearrange("p a b -> p (a b)"),
                        ptb[:, :]))
                    v.wait_ge(sT.h, b_sT + 2 * c + 2)
                    py = ps0[:, 1024 * (c % 2):1024 * (c % 2) + 1024]
                    sU.inc(v.scalar_tensor_tensor(X[:, c, :], py[:, :], 0.5,
                                                  X[:, c, :],
                                                  op0=ALU.mult, op1=ALU.add))

    browb_r = [U5[0:1, 2048 + 512 * i:2048 + 512 * (i + 1)].bitcast(BF16)
               for i in range(3)]
    brow_st = U5[0:1, 0:1024]
    normbc_v = U5[:, 3584:4608]
    bobc_v = U5[:, 4608:5632]
    wst_v = U5[:, 0:2048].rearrange("p (a b) -> p a b", a=2)
    P_sb = U5[:, 0:1024].bitcast(BF16)
    PT_sb = U5[:, 1024:2048].bitcast(BF16)
    w_a = U4[:, 0:4096].bitcast(BF16).rearrange("p (a b) -> p a b", a=8)
    w_b = U4[:, 4096:8192].bitcast(BF16).rearrange("p (a b) -> p a b", a=8)
    w_c, w_d = w_a, w_b
    at_xf = U6[:, 0:1024]
    at_xb = U6[:, 1024:1536].bitcast(BF16)
    at_xfm = U6[:, 1536:2048].bitcast(BF16).rearrange("p (a b) -> p a b", a=8)
    at_q = U6[:, 2048:2560].bitcast(BF16).rearrange("p (a b) -> p a b", a=8)
    at_o = U6[:, 2560:3072].bitcast(BF16)
    at_ofm = U6[:, 3072:3584].bitcast(BF16).rearrange("p (a b) -> p a b", a=8)
    K_fm = U2[:, :].bitcast(BF16).rearrange("p (a b) -> p a b", a=8)
    V_tm = U3[:, :].bitcast(BF16).rearrange("p (a b) -> p a b", a=16)

    def load_weight_pair(wd1, dst1, wd2, dst2, with_rows, l1):
        with nc.Block() as blk:
            b_dC, b_dD, b_sV = dC.n, dD.n, sV.n

            @blk.sync
            def _(e):
                for j in range(16):
                    wd, dst = (wd1, dst1) if j < 8 else (wd2, dst2)
                    jj = j % 8
                    if j >= 2:
                        e.wait_ge(sV.h, b_sV + j - 1)
                    dpar = dC if j % 2 == 0 else dD
                    dpar.inc(e.dma_start(wst_v[:, j % 2, :],
                                         wd[jj * 128:(jj + 1) * 128, :]), 16)
                if with_rows:
                    b_sV2 = sV.n
                    for i, rd in enumerate([bv0r_d, bv1r_d, bo1r_d]):
                        e.wait_ge(sV.h, b_sV2 + 16 + i)
                        dB.inc(e.dma_start(brow_st[:, :], rd.ap()), 16)
                    dA.inc(e.dma_start(normbc_v[0:1, :], normw_d.ap()), 16)
                    dA.inc(e.dma_start(bobc_v[0:1, :], bo0r_d.ap()), 16)
                e.wait_ge(dA.h, dA.n)
                e.wait_ge(dB.h, dB.n)
            @blk.vector
            def _(v):
                for j in range(16):
                    dst = dst1 if j < 8 else dst2
                    v.wait_ge((dC if j % 2 == 0 else dD).h,
                              (b_dC if j % 2 == 0 else b_dD) + 16 * (j // 2 + 1))
                    sV.inc(v.tensor_copy(dst[:, j % 8, :], wst_v[:, j % 2, :]))
                if with_rows:
                    b_dB2 = dB.n - 48
                    for i in range(3):
                        v.wait_ge(dB.h, b_dB2 + 16 * (i + 1))
                        sV.inc(v.tensor_copy(browb_r[i][:, :], brow_st[:, :]))
            if with_rows:
                @blk.gpsimd
                def _(g):
                    g.wait_ge(dA.h, dA.n)
                    g.partition_broadcast(normbc_v[:, :], normbc_v[:, :],
                                          channels=128)
                    if not l1:
                        g.partition_broadcast(bobc_v[:, :], bobc_v[:, :],
                                              channels=128)

    def attn(l1):
        ln_alpha(with_rms=not l1)
        scl = scaleB if not l1 else scaleA
        bia = biasB if not l1 else biasA
        wt = wt1_d if l1 else wt0_d
        qsc_i = 2 if l1 else 0
        ksc_i = 3 if l1 else 1
        vrow = 1 if l1 else 0
        qscale = (1024.0 ** -0.5) if l1 else 0.125
        xb2 = [U6[:, 0:512].bitcast(BF16), U6[:, 512:1024].bitcast(BF16)]
        q2 = [U6[:, 2048:2560].bitcast(BF16).rearrange("p (a b) -> p a b", a=8),
              U6[:, 3584:4096].bitcast(BF16).rearrange("p (a b) -> p a b", a=8)]
        load_weight_pair(wt["k"], w_a, wt["v"], w_b, with_rows=True, l1=l1)
        with nc.Block() as blk:
            for c in range(NC):
                @blk.scalar
                def _(s, c=c):
                    s.wait_ge(sV.h, sV.n)
                    s.wait_ge(sT.h, sT.n)
                    sS.inc(s.activation(xb2[c % 2][:, :], X[:, c, :],
                                        AF.Identity, bias=bia[:, c:c + 1],
                                        scale=scl[:, c:c + 1]))
                @blk.vector
                def _(v, c=c):
                    v.wait_ge(sS.h, sS.n)
                    if not l1:
                        sV.inc(v.tensor_tensor(xb2[c % 2][:, :], xb2[c % 2][:, :],
                                               normbc_v[:, :], op=ALU.mult))
                    else:
                        sV.inc(v.memset(fdum[0:1, 0:1], 0.0))
                @blk.tensor
                def _(t, c=c):
                    t.wait_ge(sV.h, sV.n)
                    ptb = ps1[:, 1024:1536].bitcast(BF16)
                    for j in range(8):
                        ins = t.transpose(ptb[:, j * 128:(j + 1) * 128],
                                          xb2[c % 2][:, j * 128:(j + 1) * 128],
                                          identb[:, :])
                    sT.inc(ins)
                @blk.vector
                def _(v, c=c):
                    v.wait_ge(sT.h, sT.n)
                    ptb = ps1[:, 1024:1536].bitcast(BF16)
                    sV.inc(v.tensor_copy(
                        at_xfm[:, :, :].rearrange("p a b -> p (a b)"), ptb[:, :]))
                for dj in range(8):
                    @blk.tensor
                    def _(t, c=c, dj=dj):
                        t.wait_ge(sV.h, sV.n)
                        t.wait_ge(sS.h, sS.n)
                        psK = ps0[:, 512 * (dj % 2):512 * (dj % 2) + 128]
                        for kj in range(8):
                            ins = t.matmul(psK,
                                           w_a[:, kj, dj * 128:(dj + 1) * 128],
                                           at_xfm[:, kj, :],
                                           start=(kj == 0), stop=(kj == 7))
                        sT.inc(ins)
                    @blk.scalar
                    def _(s, c=c, dj=dj):
                        s.wait_ge(sT.h, sT.n)
                        psK = ps0[:, 512 * (dj % 2):512 * (dj % 2) + 128]
                        sS.inc(s.activation(K_fm[:, dj, c * 128:(c + 1) * 128],
                                            psK, AF.Identity,
                                            bias=bfms[:, ksc_i, dj:dj + 1]))
                @blk.tensor
                def _(t, c=c):
                    t.wait_ge(sV.h, sV.n)
                    psV = ps0[:, 1024:2048]
                    for kj in range(8):
                        for n in range(2):
                            t.matmul(psV[:, n * 512:(n + 1) * 512],
                                     at_xfm[:, kj, :],
                                     w_b[:, kj, n * 512:(n + 1) * 512],
                                     start=(kj == 0), stop=False)
                    for n in range(2):
                        ins = t.matmul(psV[:, n * 512:(n + 1) * 512],
                                       onesb[0:1, :],
                                       browb_r[vrow][:, n * 512:(n + 1) * 512],
                                       start=False, stop=True)
                    sT.inc(ins)
                @blk.vector
                def _(v, c=c):
                    v.wait_ge(sT.h, sT.n)
                    sV.inc(v.tensor_copy(V_tm[:, c, :], ps0[:, 1024:2048]))
        load_weight_pair(wt["q"], w_c, wt["o"], w_d, with_rows=False, l1=l1)
        nheads = 1 if l1 else 16
        hd = 1024 if l1 else 64
        with nc.Block() as blk:
            qev = {0: sS.n, 1: sS.n}
            def prep_steps(cc):
                steps = []
                ps = {}
                def p_apply():
                    @blk.scalar
                    def _(s):
                        s.wait_ge(sU.h, sU.n)
                        sS.inc(s.activation(xb2[cc % 2][:, :], X[:, cc, :],
                                            AF.Identity, bias=bia[:, cc:cc + 1],
                                            scale=scl[:, cc:cc + 1]))
                    ps["apply"] = sS.n
                steps.append(p_apply)
                def p_norm():
                    @blk.vector
                    def _(v):
                        v.wait_ge(sS.h, ps["apply"])
                        if not l1:
                            sV.inc(v.tensor_tensor(xb2[cc % 2][:, :],
                                                   xb2[cc % 2][:, :],
                                                   normbc_v[:, :], op=ALU.mult))
                        else:
                            sV.inc(v.memset(fdum[0:1, 0:1], 0.0))
                    ps["norm"] = sV.n
                steps.append(p_norm)
                def p_trans():
                    @blk.tensor
                    def _(t):
                        t.wait_ge(sV.h, ps["norm"])
                        t.wait_ge(sS.h, max(qev[0], qev[1]))
                        ptb = ps1[:, 1024:1536].bitcast(BF16)
                        for j in range(8):
                            ins = t.transpose(ptb[:, j * 128:(j + 1) * 128],
                                              xb2[cc % 2][:, j * 128:(j + 1) * 128],
                                              identb[:, :])
                        sT.inc(ins)
                    ps["trans"] = sT.n
                steps.append(p_trans)
                def p_evac():
                    @blk.vector
                    def _(v):
                        v.wait_ge(sT.h, ps["trans"])
                        ptb = ps1[:, 1024:1536].bitcast(BF16)
                        sV.inc(v.tensor_copy(
                            at_xfm[:, :, :].rearrange("p a b -> p (a b)"),
                            ptb[:, :]))
                    ps["evac"] = sV.n
                steps.append(p_evac)
                for dj in range(8):
                    def p_q(dj=dj):
                        @blk.tensor
                        def _(t):
                            t.wait_ge(sV.h, ps["evac"])
                            t.wait_ge(sS.h, max(qev[0], qev[1]))
                            psQ = ps1[:, 1024 + 256 * (dj % 2):
                                      1152 + 256 * (dj % 2)]
                            for kj in range(8):
                                ins = t.matmul(psQ,
                                               w_c[:, kj, dj * 128:(dj + 1) * 128],
                                               at_xfm[:, kj, :],
                                               start=(kj == 0), stop=(kj == 7))
                            sT.inc(ins)
                        qmm = sT.n
                        @blk.scalar
                        def _(s):
                            s.wait_ge(sT.h, qmm)
                            psQ = ps1[:, 1024 + 256 * (dj % 2):
                                      1152 + 256 * (dj % 2)]
                            sS.inc(s.activation(q2[cc % 2][:, dj, :], psQ,
                                                AF.Identity, scale=qscale,
                                                bias=bfms[:, qsc_i, dj:dj + 1]))
                        qev[dj % 2] = sS.n
                    steps.append(p_q)
                return steps

            pending = prep_steps(0)
            for st in pending:
                st()
            exp_par = {0: sS.n, 1: sS.n}
            for c in range(NC):
                q_ready = sS.n
                pending = prep_steps(c + 1) if c + 1 < NC else []
                split = (not l1) and c < 8
                kl = (c + 1) * 128
                if split:
                    exp_cnt = {}
                    ev_cnt = {}
                    ev_seed = sV.n

                    def em_scores(h):
                        par = h % 2
                        base = 1024 * par
                        part = 64 * (h % 2)
                        dj0 = h // 2
                        @blk.tensor
                        def _(t, c=c, h=h, part=part, dj0=dj0, base=base,
                              w1=max(q_ready, exp_par[par])):
                            t.wait_ge(sS.h, w1)
                            t.wait_ge(sU.h, sU.n)
                            for a in range(0, kl, 512):
                                b = min(a + 512, kl)
                                ins = t.matmul(ps0[:, base + a:base + b],
                                               q2[c % 2][part:part + 64, dj0, :],
                                               K_fm[part:part + 64, dj0, a:b],
                                               start=True, stop=True)
                            sT.inc(ins)
                        @blk.vector
                        def _(v, base=base, par=par):
                            v.wait_ge(sT.h, sT.n)
                            v.tensor_tensor(ps0[:, base + kl - 128:base + kl],
                                            ps0[:, base + kl - 128:base + kl],
                                            causal[:, :], op=ALU.add)
                            sW.inc(v.memset(fdum[0:1, 0:1], 0.0))
                            v.wait_ge(sW.h, sW.n)
                            sV.inc(v.tensor_reduce(nmax[:, par:par + 1],
                                                   ps0[:, base:base + kl],
                                                   axis=AX.X, op=ALU.max,
                                                   negate=True))

                    def em_exp(h):
                        par = h % 2
                        base = 1024 * par
                        @blk.scalar
                        def _(s, h=h, base=base, par=par):
                            s.wait_ge(sV.h, sV.n)
                            sS.inc(s.activation(P_sb[:, base:base + kl],
                                                ps0[:, base:base + kl], AF.Exp,
                                                bias=nmax[:, par:par + 1],
                                                accum_out=rs16[:, h:h + 1]))
                        exp_cnt[h] = sS.n
                        exp_par[par] = sS.n

                    def em_ptpv(h):
                        par = h % 2
                        base = 1024 * par
                        @blk.tensor
                        def _(t, h=h, base=base, we=exp_cnt[h],
                              wv=ev_cnt.get(h - 2, ev_seed)):
                            t.wait_ge(sS.h, we)
                            t.wait_ge(sV.h, wv)
                            psPT = ps1[:, 1536:2048].bitcast(BF16)
                            for j in range(c + 1):
                                ins = t.transpose(
                                    psPT[:, j * 128:(j + 1) * 128],
                                    P_sb[:, base + j * 128:base + (j + 1) * 128],
                                    identb[:, :])
                            sT.inc(ins)
                        @blk.vector
                        def _(v, base=base):
                            v.wait_ge(sT.h, sT.n)
                            psPT = ps1[:, 1536:2048].bitcast(BF16)
                            sV.inc(v.tensor_copy(
                                PT_sb[:, base:base + kl],
                                psPT[:, 0:kl]))
                        ev_cnt[h] = sV.n
                        @blk.tensor
                        def _(t, c=c, h=h, base=base, wv=ev_cnt[h]):
                            t.wait_ge(sV.h, wv)
                            for j in range(c + 1):
                                ins = t.matmul(ps1[:, h * 64:(h + 1) * 64],
                                               PT_sb[:, base + j * 128:
                                                     base + (j + 1) * 128],
                                               V_tm[:, j, h * 64:(h + 1) * 64],
                                               start=(j == 0), stop=(j == c))
                            sT.inc(ins)

                    for h in range(nheads):
                        em_scores(h)
                        em_exp(h)
                        if h > 0:
                            em_ptpv(h - 1)
                        if pending:
                            pending.pop(0)()
                    em_ptpv(nheads - 1)
                else:
                  exp_cnt = {}
                  mx_cnt = {}
                  ptc = {}
                  ev_s = {}
                  seed_v = sV.n
                  seed_t = sT.n

                  def sm_scores(h):
                    part = 0 if l1 else 64 * (h % 2)
                    dj0 = 0 if l1 else h // 2
                    prev_exp = exp_cnt.get(h - 1,
                                           max(exp_par[0], exp_par[1]))
                    @blk.tensor
                    def _(t, c=c, h=h, part=part, dj0=dj0, kl=kl,
                          w1=max(q_ready, prev_exp)):
                        t.wait_ge(sS.h, w1)
                        t.wait_ge(sU.h, sU.n)
                        for a in range(0, kl, 512):
                            b = min(a + 512, kl)
                            if l1:
                                for kj in range(8):
                                    ins = t.matmul(ps0[:, a:b],
                                                   q2[c % 2][:, kj, :],
                                                   K_fm[:, kj, a:b],
                                                   start=(kj == 0),
                                                   stop=(kj == 7))
                            else:
                                ins = t.matmul(
                                    ps0[:, a:b],
                                    q2[c % 2][part:part + 64, dj0, :],
                                    K_fm[part:part + 64, dj0, a:b],
                                    start=True, stop=True)
                        sT.inc(ins)
                    @blk.vector
                    def _(v, c=c, kl=kl):
                        v.wait_ge(sT.h, sT.n)
                        v.tensor_tensor(ps0[:, kl - 128:kl], ps0[:, kl - 128:kl],
                                        causal[:, :], op=ALU.add)
                        sW.inc(v.memset(fdum[0:1, 0:1], 0.0))
                        v.wait_ge(sW.h, sW.n)
                        sV.inc(v.tensor_reduce(nmax[:, 0:1], ps0[:, 0:kl],
                                               axis=AX.X, op=ALU.max,
                                               negate=True))
                    mx_cnt[h] = sV.n

                  def sm_exp(h):
                    @blk.scalar
                    def _(s, c=c, h=h, kl=kl, wm=mx_cnt[h],
                          wt=ptc.get(h - 1, seed_t)):
                        s.wait_ge(sV.h, wm)
                        s.wait_ge(sT.h, wt)
                        sS.inc(s.activation(P_sb[:, 0:kl], ps0[:, 0:kl], AF.Exp,
                                            bias=nmax[:, 0:1],
                                            accum_out=rs16[:, h:h + 1]))
                    exp_cnt[h] = sS.n
                    exp_par[0] = sS.n

                  def sm_ptpv(h):
                    for ggi in range((c + 8) // 8):
                        j0, j1 = ggi * 8, min(ggi * 8 + 8, c + 1)
                        wv = (ev_s.get(h - 1, seed_v) if ggi == 0
                              else ev_s[h])
                        @blk.tensor
                        def _(t, j0=j0, j1=j1, we=exp_cnt[h], wv=wv):
                            t.wait_ge(sS.h, we)
                            t.wait_ge(sV.h, wv)
                            psPT = ps1[:, 1536:2048].bitcast(BF16)
                            for j in range(j0, j1):
                                ins = t.transpose(
                                    psPT[:, (j - j0) * 128:(j - j0 + 1) * 128],
                                    P_sb[:, j * 128:(j + 1) * 128],
                                    identb[:, :])
                            sT.inc(ins)
                        tg = sT.n
                        @blk.vector
                        def _(v, j0=j0, j1=j1, tg=tg):
                            v.wait_ge(sT.h, tg)
                            psPT = ps1[:, 1536:2048].bitcast(BF16)
                            sV.inc(v.tensor_copy(
                                PT_sb[:, j0 * 128:j1 * 128],
                                psPT[:, 0:(j1 - j0) * 128]))
                        ev_s[h] = sV.n
                    ptc[h] = sT.n
                    @blk.tensor
                    def _(t, c=c, h=h, wv=ev_s[h]):
                        t.wait_ge(sV.h, wv)
                        if l1:
                            for n in range(2):
                                for j in range(c + 1):
                                    ins = t.matmul(
                                        ps1[:, n * 512:(n + 1) * 512],
                                        PT_sb[:, j * 128:(j + 1) * 128],
                                        V_tm[:, j, n * 512:(n + 1) * 512],
                                        start=(j == 0), stop=(j == c))
                        else:
                            for j in range(c + 1):
                                ins = t.matmul(ps1[:, h * 64:(h + 1) * 64],
                                               PT_sb[:, j * 128:(j + 1) * 128],
                                               V_tm[:, j, h * 64:(h + 1) * 64],
                                               start=(j == 0), stop=(j == c))
                        sT.inc(ins)

                  for h in range(nheads):
                    sm_scores(h)
                    if h > 0:
                        sm_ptpv(h - 1)
                    sm_exp(h)
                    if pending and not l1:
                        pending.pop(0)()
                  sm_ptpv(nheads - 1)
                @blk.vector
                def _(v, c=c, nheads=nheads, hd=hd):
                    v.wait_ge(sT.h, sT.n)
                    v.wait_ge(sS.h, sS.n)
                    v.reciprocal(rc16[:, 0:nheads], rs16[:, 0:nheads])
                    sW.inc(v.memset(fdum[0:1, 0:1], 0.0))
                    v.wait_ge(sW.h, sW.n)
                    for h in range(nheads):
                        ins = v.tensor_scalar(at_o[:, h * hd:(h + 1) * hd],
                                              ps1[:, h * hd:(h + 1) * hd],
                                              rc16[:, h:h + 1], None,
                                              op0=ALU.mult)
                    sV.inc(ins)
                @blk.tensor
                def _(t, c=c):
                    t.wait_ge(sV.h, sV.n)
                    t.wait_ge(sS.h, sS.n)
                    ptb = ps1[:, 1024:1536].bitcast(BF16)
                    for j in range(8):
                        ins = t.transpose(ptb[:, j * 128:(j + 1) * 128],
                                          at_o[:, j * 128:(j + 1) * 128],
                                          identb[:, :])
                    sT.inc(ins)
                @blk.vector
                def _(v):
                    v.wait_ge(sT.h, sT.n)
                    ptb = ps1[:, 1024:1536].bitcast(BF16)
                    sV.inc(v.tensor_copy(
                        at_ofm[:, :, :].rearrange("p a b -> p (a b)"), ptb[:, :]))
                @blk.tensor
                def _(t, c=c, l1=l1):
                    t.wait_ge(sV.h, sV.n)
                    psY = ps0[:, 0:1024]
                    for j in range(8):
                        for n in range(2):
                            ins = t.matmul(psY[:, n * 512:(n + 1) * 512],
                                           at_ofm[:, j, :],
                                           w_d[:, j, n * 512:(n + 1) * 512],
                                           start=(j == 0),
                                           stop=(j == 7 and not l1))
                    if l1:
                        for n in range(2):
                            ins = t.matmul(psY[:, n * 512:(n + 1) * 512],
                                           onesb[0:1, :],
                                           browb_r[2][:, n * 512:(n + 1) * 512],
                                           start=False, stop=True)
                    sT.inc(ins)
                @blk.vector
                def _(v, c=c, l1=l1):
                    def vf(instr):
                        sW.inc(instr)
                        v.wait_ge(sW.h, sW.n)
                    v.wait_ge(sT.h, sT.n)
                    psY = ps0[:, 0:1024]
                    vf(v.tensor_tensor(X[:, c, :], X[:, c, :], psY[:, :],
                                       op=ALU.add))
                    if not l1:
                        vf(v.tensor_tensor(X[:, c, :], X[:, c, :],
                                           xb2[c % 2][:, :], op=ALU.add))
                        v.tensor_tensor(X[:, c, :], X[:, c, :], bobc_v[:, :],
                                        op=ALU.add)
                    sU.inc(v.memset(fdum[0:1, 1:2], 0.0))
                for st in pending:
                    st()
                pending = []

    wi_v = U2[:, :].bitcast(BF16).rearrange("p (a b) -> p a b", a=8)
    wo_v = U3[:, :].bitcast(BF16).rearrange("p (a b) -> p a b", a=16)
    m_v = U4[:, :].bitcast(BF16).rearrange("p (a b) -> p a b", a=8)
    pk_st = U5[:, 0:2048]
    gbuf = [U5[:, 0:3072], U5[:, 3072:6144],
            X[:, 8:11, :].rearrange("p a b -> p (a b)"),
            X[:, 11:14, :].rearrange("p a b -> p (a b)")]
    pk_xb = [U6[:, 0:512].bitcast(BF16), U6[:, 512:1024].bitcast(BF16)]
    pk_xfm = [U6[:, 1024 + 512 * i:1536 + 512 * i].bitcast(BF16)
              .rearrange("p (a b) -> p a b", a=8) for i in range(2)]
    pk_htm = U6[:, 2048:3072].bitcast(BF16)
    pk_hfm = U6[:, 3072:4096].bitcast(BF16).rearrange("p (a b) -> p a b", a=16)


    def pkm():
        ln_alpha(with_rms=False)
        with nc.Block() as blk:
            b_sV = sV.n
            b_dC1, b_dD1 = dC.n, dD.n
            @blk.sync
            def _(e):
                for j in range(8):
                    if j >= 1:
                        e.wait_ge(sV.h, b_sV + j)
                    dpar = dC if j % 2 == 0 else dD
                    dpar.inc(e.dma_start(pk_st[:, :],
                                         wint_d[j * 128:(j + 1) * 128, :]), 16)
            @blk.vector
            def _(v):
                for j in range(8):
                    bb = b_dC1 if j % 2 == 0 else b_dD1
                    v.wait_ge((dC if j % 2 == 0 else dD).h, bb + 16 * (j // 2 + 1))
                    sV.inc(v.tensor_copy(wi_v[:, j, :], pk_st[:, :]))
        with nc.Block() as blk:
            b_sV2 = sV.n
            b_dC2, b_dD2 = dC.n, dD.n
            @blk.sync
            def _(e):
                for j in range(16):
                    if j >= 1:
                        e.wait_ge(sV.h, b_sV2 + j)
                    dpar = dC if j % 2 == 0 else dD
                    dpar.inc(e.dma_start(pk_st[:, 0:1024],
                                         woutt_d[j * 128:(j + 1) * 128, :]), 16)
                for j in range(8):
                    dA.inc(e.dma_start(m_v[:, j, :],
                                       mdram[j * 128:(j + 1) * 128, :]), 16)
                e.wait_ge(dA.h, dA.n)
            @blk.vector
            def _(v):
                for j in range(16):
                    bb = b_dC2 if j % 2 == 0 else b_dD2
                    v.wait_ge((dC if j % 2 == 0 else dD).h, bb + 16 * (j // 2 + 1))
                    sV.inc(v.tensor_copy(wo_v[:, j, :], pk_st[:, 0:1024]))
        with nc.Block() as blk:
            swap_in = {}
            spill_a = {}
            xup_s = {}
            @blk.sync
            def _(e):
                for j in range(8):
                    dB.inc(e.dma_start(xsp1[j, :, :], X[:, 8 + j, :]), 16)
            spill_a["dB"] = dB.n
            gk = {}
            gsnap = {}
            stw = {}
            gelu_s = {}
            csnap = {}
            dsnap = {}
            tsnap = {}

            def emit_gather(k, c, h, table):
                par = c % 2
                dp = (dG, dH, dI, dJ)[k % 4]
                @blk.gpsimd
                def _(g):
                    g.wait_ge(dp.h, dp.n)
                    if k - 4 in gsnap:
                        vv, tt, ss, ww = gsnap[k - 4]
                        g.wait_ge(sV.h, vv)
                        g.wait_ge(sT.h, tt)
                        g.wait_ge(sS.h, ss)
                        g.wait_ge(sW.h, ww)
                    else:
                        g.wait_ge(sV.h, sV.n)
                    if k % 4 >= 2 and k < 4:
                        g.wait_ge(dB.h, spill_a["dB"])
                    dp.inc(g.dma_gather(
                        gbuf[k % 4].unsqueeze(1), table.ap(),
                        wrap16[:, par, 8 * h:8 * h + 8], 128, gnum, 3072), 16)
                gk[k] = dp.n

            def emit_in_dot(k, c, h, last):
                par = c % 2
                dp = (dG, dH, dI, dJ)[k % 4]
                gb = gbuf[k % 4]
                @blk.scalar
                def _(s):
                    s.wait_ge(dp.h, gk[k])
                    if k - 4 in tsnap:
                        s.wait_ge(sT.h, tsnap[k - 4])
                    if k - 4 in stw:
                        s.wait_ge(sW.h, stw[k - 4])
                    sS.inc(s.activation(gb[:, 0:1024].bitcast(BF16),
                                        gb[:, 0:2048], AF.Identity))
                csnap[k] = sS.n
                @blk.vector
                def _(v):
                    v.wait_ge(dp.h, gk[k])
                    if k - 1 in dsnap:
                        v.wait_ge(sV.h, dsnap[k - 1])
                    sW.inc(v.scalar_tensor_tensor(
                        gb[:, 2048:3072], gb[:, 2048:3072], gate8[:, par, h:h + 1],
                        pk_xb[par][:, :], op0=ALU.mult, op1=ALU.mult,
                        accum_out=lin1[:, :]))
                    v.wait_ge(sW.h, sW.n)
                    if k - 1 in tsnap:
                        v.wait_ge(sT.h, tsnap[k - 1])
                    sV.inc(v.tensor_scalar(diagt[:, 0:64].bitcast(BF16),
                                           i128f[:, :], lin1[:, 0:1], None,
                                           op0=ALU.mult))
                stw[k] = sW.n
                dsnap[k] = sV.n
                @blk.tensor
                def _(t):
                    t.wait_ge(sV.h, dsnap[k])
                    t.wait_ge(sS.h, csnap[k])
                    gb16 = gb[:, 0:1024].bitcast(BF16)
                    for a in range(0, 2048, 512):
                        ins = t.matmul(ps0[:, a:a + 512],
                                       diagt[:, 0:64].bitcast(BF16),
                                       gb16[:, a:a + 512],
                                       start=False, stop=last)
                    sT.inc(ins)
                tsnap[k] = sT.n
                gsnap[k] = (sV.n, sT.n, sS.n, sW.n)

            def emit_out_dot(k, c, h, last):
                par = c % 2
                dp = (dG, dH, dI, dJ)[k % 4]
                gb = gbuf[k % 4]
                @blk.scalar
                def _(s):
                    s.wait_ge(dp.h, gk[k])
                    if k - 4 in tsnap:
                        s.wait_ge(sT.h, tsnap[k - 4])
                    if k - 4 in stw:
                        s.wait_ge(sW.h, stw[k - 4])
                    sS.inc(s.activation(gb[:, 0:512].bitcast(BF16),
                                        gb[:, 0:1024], AF.Identity))
                csnap[k] = sS.n
                @blk.vector
                def _(v):
                    v.wait_ge(dp.h, gk[k])
                    if k - 1 in dsnap:
                        v.wait_ge(sV.h, dsnap[k - 1])
                    v.wait_ge(sS.h, gelu_s[c])
                    sW.inc(v.scalar_tensor_tensor(
                        gb[:, 1024:3072], gb[:, 1024:3072], gate8[:, par, h:h + 1],
                        pk_htm[:, :], op0=ALU.mult, op1=ALU.mult,
                        accum_out=lin1[:, :]))
                    v.wait_ge(sW.h, sW.n)
                    if k - 1 in tsnap:
                        v.wait_ge(sT.h, tsnap[k - 1])
                    sV.inc(v.tensor_scalar(diagt[:, 0:64].bitcast(BF16),
                                           i128f[:, :], lin1[:, 0:1], None,
                                           op0=ALU.mult))
                stw[k] = sW.n
                dsnap[k] = sV.n
                @blk.tensor
                def _(t):
                    t.wait_ge(sV.h, dsnap[k])
                    t.wait_ge(sS.h, csnap[k])
                    gb16 = gb[:, 0:512].bitcast(BF16)
                    for n in range(2):
                        ins = t.matmul(ps0[:, n * 512:(n + 1) * 512],
                                       diagt[:, 0:64].bitcast(BF16),
                                       gb16[:, n * 512:(n + 1) * 512],
                                       start=False, stop=last)
                    sT.inc(ins)
                tsnap[k] = sT.n
                gsnap[k] = (sV.n, sT.n, sS.n, sW.n)

            def emit_prelude(c):
                par = c % 2
                sl = c if c < 8 else c - 8
                @blk.scalar
                def _(s):
                    if c >= 8:
                        s.wait_ge(dA.h, swap_in[sl])
                    sS.inc(s.activation(pk_xb[par][:, :], X[:, sl, :],
                                        AF.Identity, bias=biasA[:, c:c + 1],
                                        scale=scaleA[:, c:c + 1]))
                @blk.tensor
                def _(t):
                    t.wait_ge(sS.h, sS.n)
                    t.wait_ge(sV.h, sV.n)
                    ptb = ps1[:, 1024:1536].bitcast(BF16)
                    for j in range(8):
                        ins = t.transpose(ptb[:, j * 128:(j + 1) * 128],
                                          pk_xb[par][:, j * 128:(j + 1) * 128],
                                          identb[:, :])
                    sT.inc(ins)
                @blk.vector
                def _(v):
                    v.wait_ge(sT.h, sT.n)
                    ptb = ps1[:, 1024:1536].bitcast(BF16)
                    sV.inc(v.tensor_copy(
                        pk_xfm[par][:, :, :].rearrange("p a b -> p (a b)"),
                        ptb[:, :]))
                @blk.tensor
                def _(t):
                    t.wait_ge(sV.h, sV.n)
                    for j in range(8):
                        for a in range(0, 2048, 512):
                            ins = t.matmul(ps1[:, a:a + 512],
                                           pk_xfm[par][:, j, :],
                                           m_v[:, j, a:a + 512],
                                           start=(j == 0), stop=(j == 7))
                    sT.inc(ins)
                sims3 = ps1[:, 0:2048].rearrange("p (a b) -> p a b", a=16)
                @blk.vector
                def _(v):
                    def vfw(instr):
                        sW.inc(instr)
                        v.wait_ge(sW.h, sW.n)
                    v.wait_ge(sT.h, sT.n)
                    vfw(v.tensor_reduce(m16[:, :], sims3, axis=AX.X, op=ALU.max))
                    mbc = m16[:, :].unsqueeze(2).broadcast_to([128, 16, 128])
                    vfw(v.tensor_tensor(sims3, sims3, mbc, op=ALU.is_equal))
                    ibc = iotaB[:, :].unsqueeze(1).broadcast_to([128, 16, 128])
                    vfw(v.scalar_tensor_tensor(sims3, sims3, -256.0, ibc,
                                               op0=ALU.mult, op1=ALU.add))
                    vfw(v.tensor_reduce(a16[:, :], sims3, axis=AX.X, op=ALU.min))
                    vfw(v.tensor_copy(a16b_v[:, :], a16[:, :]))
                    vfw(v.tensor_tensor(gate8[:, par, :], m16[:, 0:8],
                                        m16[:, 8:16], op=ALU.add))
                    sV.inc(v.tensor_scalar_max(gate8[:, par, :],
                                               gate8[:, par, :], 0.0))
                pw0 = ps1[:, 0:64].rearrange("p (a b) -> p a b", a=8)
                pw1 = ps1[:, 64:128].rearrange("p (a b) -> p a b", a=8)
                @blk.tensor
                def _(t):
                    t.wait_ge(sV.h, sV.n)
                    for r in range(8):
                        t.matmul(pw0[:, :, r], permrep[:, r, :], a16b_v[:, 0:8],
                                 start=True, stop=True)
                        ins = t.matmul(pw1[:, :, r], permrep[:, r, :],
                                       a16b_v[:, 8:16], start=True, stop=True)
                    sT.inc(ins)
                @blk.vector
                def _(v):
                    v.wait_ge(sT.h, sT.n)
                    sW.inc(v.tensor_copy(
                        wtmp_v[:, :], pw0[:, :, :].rearrange("p a b -> p (a b)")))
                    v.wait_ge(sW.h, sW.n)
                    sV.inc(v.scalar_tensor_tensor(
                        wrap16[:, par, :], wtmp_v[:, :], 128.0,
                        pw1[:, :, :].rearrange("p a b -> p (a b)"),
                        op0=ALU.mult, op1=ALU.add))

            def emit_win(c):
                par = c % 2
                @blk.tensor
                def _(t):
                    t.wait_ge(sU.h, sU.n)
                    t.wait_ge(sV.h, sV.n)
                    for j in range(8):
                        for a in range(0, 2048, 512):
                            ins = t.matmul(ps0[:, a:a + 512],
                                           pk_xfm[par][:, j, :],
                                           wi_v[:, j, a:a + 512],
                                           start=(j == 0), stop=False)
                    sT.inc(ins)

            state = {"next": 0}

            def pump(upto):
                while state["next"] <= min(upto, 16 * NC - 1):
                    k = state["next"]
                    kc, kh = k // 16, k % 16
                    if kh < 8:
                        emit_gather(k, kc, kh, tin_d)
                    else:
                        emit_gather(k, kc, kh - 8, tout_d)
                    state["next"] += 1

            emit_prelude(0)
            emit_win(0)
            pump(3)
            for c in range(NC):
                for h in range(8):
                    emit_in_dot(16 * c + h, c, h, last=(h == 7))
                    pump(16 * c + h + 4)
                if c + 1 < NC:
                    emit_prelude(c + 1)
                @blk.scalar
                def _(s, c=c):
                    s.wait_ge(sT.h, sT.n)
                    sS.inc(s.activation(pk_htm[:, :], ps0[:, 0:2048], AF.Gelu))
                gelu_s[c] = sS.n
                for gg in range(2):
                    @blk.tensor
                    def _(t, gg=gg):
                        t.wait_ge(sS.h, sS.n)
                        t.wait_ge(sV.h, sV.n)
                        ptb = ps0[:, 1536:2048].bitcast(BF16)
                        for j in range(8):
                            ins = t.transpose(
                                ptb[:, j * 128:(j + 1) * 128],
                                pk_htm[:, (gg * 8 + j) * 128:
                                       (gg * 8 + j + 1) * 128],
                                identb[:, :])
                        sT.inc(ins)
                    @blk.vector
                    def _(v, gg=gg):
                        v.wait_ge(sT.h, sT.n)
                        ptb = ps0[:, 1536:2048].bitcast(BF16)
                        sV.inc(v.tensor_copy(
                            pk_hfm[:, gg * 8:gg * 8 + 8, :].rearrange(
                                "p a b -> p (a b)"), ptb[:, :]))
                @blk.tensor
                def _(t):
                    t.wait_ge(sV.h, sV.n)
                    for j in range(16):
                        for n in range(2):
                            ins = t.matmul(ps0[:, n * 512:(n + 1) * 512],
                                           pk_hfm[:, j, :],
                                           wo_v[:, j, n * 512:(n + 1) * 512],
                                           start=(j == 0), stop=False)
                    sT.inc(ins)
                for h in range(8):
                    emit_out_dot(16 * c + 8 + h, c, h, last=(h == 7))
                    pump(16 * c + 8 + h + 4)
                sl = c if c < 8 else c - 8
                @blk.vector
                def _(v, c=c, sl=sl):
                    v.wait_ge(sT.h, sT.n)
                    sU.inc(v.scalar_tensor_tensor(
                        X[:, sl, :], ps0[:, 0:1024], 0.5, X[:, sl, :],
                        op0=ALU.mult, op1=ALU.add))
                xup_s[c] = sU.n
                @blk.sync
                def _(e, c=c, sl=sl):
                    if c < 8:
                        e.wait_ge(sU.h, xup_s[c])
                        dB.inc(e.dma_start(xsp2[sl, :, :], X[:, sl, :]), 16)
                        e.wait_ge(dB.h, dB.n)
                        e.wait_ge(dA.h, dA.n)
                        dA.inc(e.dma_start(X[:, sl, :], xsp1[sl, :, :]), 16)
                        swap_in[sl] = dA.n
                    else:
                        e.wait_ge(sU.h, xup_s[c])
                        dB.inc(e.dma_start(xsp1[sl, :, :], X[:, sl, :]), 16)
                if c + 1 < NC:
                    emit_win(c + 1)
            @blk.sync
            def _(e):
                for j in range(8):
                    e.wait_ge(dB.h, dB.n)
                for j in range(8):
                    dA.inc(e.dma_start(X[:, j, :], xsp2[j, :, :]), 16)
                e.wait_ge(sT.h, sT.n)
                e.wait_ge(sV.h, sV.n)
                e.wait_ge(sS.h, sS.n)
                for j in range(8):
                    dA.inc(e.dma_start(X[:, 8 + j, :], xsp1[j, :, :]), 16)
                e.wait_ge(dA.h, dA.n)

    def write_out():
        with nc.Block() as blk:
            @blk.sync
            def _(e):
                for c in range(NC):
                    dO.inc(e.dma_start(out_d[c * 128:(c + 1) * 128, :],
                                       X[:, c, :]), 16)
                e.wait_ge(dO.h, dO.n)

    def dump_dbg():
        if dbg_d is None:
            return
        with nc.Block() as blk:
            @blk.sync
            def _(e):
                dO.inc(e.dma_start(dbg_d[:, 0:32],
                                   mv[:, :, :].rearrange("p a b -> p (a b)")), 16)
                dO.inc(e.dma_start(dbg_d[:, 32:48], scaleA[:, :]), 16)
                dO.inc(e.dma_start(dbg_d[:, 48:64], biasA[:, :]), 16)
                e.wait_ge(dO.h, dO.n)

    if STOP >= 1:
        sign_ffn()
    dump_dbg()
    if STOP >= 2:
        attn(l1=False)
    if STOP >= 3:
        sign_ffn()
    if STOP >= 4:
        pkm()
    if STOP >= 5:
        attn(l1=True)
    if STOP >= 6:
        pkm()
    write_out()

    stack.close()
    return nc


def _host_inputs(inp):
    f32 = np.float32

    def tr(a):
        return np.ascontiguousarray(np.asarray(a, dtype=f32).T)

    def fm(b):
        return np.ascontiguousarray(np.asarray(b, dtype=f32).reshape(8, 128).T)

    def row(b):
        return np.ascontiguousarray(np.asarray(b, dtype=f32).reshape(1, D))

    com = {
        "w0t": tr(inp["L0_ffn_weight"]),
        "normw": row(inp["L0_attn_norm_w"]),
        "wqt0": tr(inp["L0_attn_Wq"]), "wkt0": tr(inp["L0_attn_Wk"]),
        "wvt0": tr(inp["L0_attn_Wv"]), "wot0": tr(inp["L0_attn_Wo"]),
        "bq0f": fm(inp["L0_attn_bq"]), "bk0f": fm(inp["L0_attn_bk"]),
        "bv0r": row(inp["L0_attn_bv"]), "bo0r": row(inp["L0_attn_bo"]),
        "wqt1": tr(inp["L1_attn_w_q"]), "wkt1": tr(inp["L1_attn_w_k"]),
        "wvt1": tr(inp["L1_attn_w_v"]), "wot1": tr(inp["L1_attn_w_o"]),
        "bq1f": fm(inp["L1_attn_b_q"]), "bk1f": fm(inp["L1_attn_b_k"]),
        "bv1r": row(inp["L1_attn_b_v"]), "bo1r": row(inp["L1_attn_b_o"]),
        "wq8k": np.ascontiguousarray(np.asarray(inp["L1_ffn_Wq"], dtype=f32)),
        "keyst": np.ascontiguousarray(
            np.asarray(inp["L1_ffn_keys"], dtype=f32).transpose(2, 0, 3, 1)),
        "wint": tr(inp["L1_ffn_W_in"]),
        "woutt": tr(inp["L1_ffn_W_out"]),
        "tin": np.ascontiguousarray(np.concatenate(
            [np.asarray(inp["L1_ffn_in_b"], dtype=f32),
             np.asarray(inp["L1_ffn_in_a"], dtype=f32)], axis=1)),
        "tout": np.ascontiguousarray(np.concatenate(
            [np.asarray(inp["L1_ffn_out_b"], dtype=f32),
             np.asarray(inp["L1_ffn_out_a"], dtype=f32)], axis=1)),
    }
    xs = np.ascontiguousarray(np.asarray(inp["L0_attn_x"], dtype=f32))
    return com, xs


def kernel(**inp):
    com, xs = _host_inputs(inp)
    nc = _build()
    import bass_rust as _br
    _br.insert_act_table_loads(nc)
    lower_extended_insts(nc)
    in_maps = [dict(com, x=np.ascontiguousarray(xs[c])) for c in range(8)]
    res = run_bass_kernel_spmd(nc, in_maps, core_ids=list(range(8)))
    return np.stack([np.asarray(res.results[c]["out"]) for c in range(8)], axis=0)
